# version 01
# baseline (speedup 1.0000x reference)
import numpy as np
import ml_dtypes
import concourse.bass as bass
import concourse.mybir as mybir
from concourse.bass_utils import run_bass_kernel_spmd

F32 = mybir.dt.float32
BF16 = mybir.dt.bfloat16
I32 = mybir.dt.int32
AF = mybir.ActivationFunctionType
ALU = mybir.AluOpType
AX = mybir.AxisListType
BF16NP = ml_dtypes.bfloat16

NCORES = 8
DM = 2048
DCH = DM // 128
B = 4
SEQ = 4096
TOK = 2048
NTT = TOK // 128



class Sem:
    __slots__ = ("h", "v")
    def __init__(self, h):
        self.h = h
        self.v = 0

    def inc(self, instr, n=1):
        instr.then_inc(self.h, n)
        self.v += n
        return self.v


class InstrProxy:
    __slots__ = ("chain",)
    def __init__(self):
        self.chain = []

    def then_inc(self, semh, n=1):
        self.chain.append((semh, n))
        return self


class EngineProxy:
    def __init__(self, name):
        object.__setattr__(self, "_name", name)
        object.__setattr__(self, "_ops", [])

    def __getattr__(self, meth):
        ops = object.__getattribute__(self, "_ops")
        def call(*a, **k):
            ip = InstrProxy()
            ops.append((meth, a, k, ip))
            return ip
        return call

    def raw(self, fn):
        object.__getattribute__(self, "_ops").append(("__thunk__", fn, None, None))


def _replay(eng, proxy):
    for meth, a, k, ip in object.__getattribute__(proxy, "_ops"):
        if meth == "__thunk__":
            a(eng)
            continue
        instr = getattr(eng, meth)(*a, **k)
        for semh, n in ip.chain:
            instr = instr.then_inc(semh, n)


class Prog:
    def __init__(self):
        self.nc = bass.Bass("TRN2", target_bir_lowering=False, detect_race_conditions=False)
        self.e = {n: EngineProxy(n) for n in ('y', 't', 'v', 's', 'g')}
        self.sems = {}
        self.barrier_k = 0

    def sem(self, name):
        if name not in self.sems:
            self.sems[name] = Sem(self.nc.alloc_semaphore(name=name))
        return self.sems[name]

    def compile_block(self):
        with self.nc.Block() as block:
            block.sync(lambda e: _replay(e, self.e['y']))
            block.tensor(lambda e: _replay(e, self.e['t']))
            block.vector(lambda e: _replay(e, self.e['v']))
            block.scalar(lambda e: _replay(e, self.e['s']))
            block.gpsimd(lambda e: _replay(e, self.e['g']))

    def vsync(self, instr, eng='v'):
        s = self.sem("selfsync_" + eng)
        s.inc(instr, 1)
        self.e[eng].wait_ge(s.h, s.v)
        return instr

    def barrier(self):
        bar = self.sem("bar")
        self.barrier_k += 1
        engs = [self.e[k] for k in ('y', 't', 'v', 's', 'g')]
        for e in engs:
            e.nop().then_inc(bar.h, 1)
        bar.v += 5
        for e in engs:
            e.wait_ge(bar.h, bar.v)


class ScopeAlloc:
    def __init__(self, nc):
        self.nc = nc
        self.ctxs = []

    def sb(self, name, shape, dt):
        c = self.nc.sbuf_tensor(name, shape, dt)
        self.ctxs.append(c)
        return c.__enter__()

    def ps(self, name, shape, dt=F32):
        c = self.nc.psum_tensor(name, shape, dt)
        self.ctxs.append(c)
        return c.__enter__()

    def close(self):
        for c in reversed(self.ctxs):
            c.__exit__(None, None, None)
        self.ctxs = []


class Ring:
    def __init__(self, p, banks, tag):
        self.banks = banks
        self.n = len(banks)
        self.i = 0
        self.j = 0
        self.mm = p.sem(tag + "_mm")
        self.ev = p.sem(tag + "_ev")
        self.mm_base = self.mm.v
        self.ev_base = self.ev.v

    def produce(self, t_eng):
        if self.i >= self.n:
            t_eng.wait_ge(self.ev.h, self.ev_base + self.i - self.n + 1)
        return self.banks[self.i % self.n]

    def mm_done(self, instr):
        self.mm.inc(instr, 1)
        self.i += 1

    def consume_wait(self, eng):
        eng.wait_ge(self.mm.h, self.mm_base + self.j + 1)
        return self.banks[self.j % self.n]

    def ev_done(self, instr):
        self.ev.inc(instr, 1)
        self.j += 1


def mm_accum(t, psum_ap, pairs):
    n = len(pairs)
    last = None
    for i, (l, r) in enumerate(pairs):
        last = t.matmul(psum_ap, l, r, start=(i == 0), stop=(i == n - 1))
    return last



def host_consts():
    return {
        "c_ident": np.eye(128, dtype=BF16NP),
        "c_ones_col": np.ones((128, 1), dtype=BF16NP),
        "c_ones_row": np.ones((1, 128), dtype=BF16NP),
        "c_neg128_row": np.full((1, 512), -128.0, dtype=BF16NP),
        "c_eps": np.tile(np.array([[1e-5, 1e-6, 1.1920928955078125e-07, 0.0]],
                                  np.float32), (128, 1)),
    }


def decl_consts(nc):
    return {
        "ident": nc.dram_tensor("c_ident", [128, 128], BF16, kind="ExternalInput").ap(),
        "ones_col": nc.dram_tensor("c_ones_col", [128, 1], BF16, kind="ExternalInput").ap(),
        "ones_row": nc.dram_tensor("c_ones_row", [1, 128], BF16, kind="ExternalInput").ap(),
        "neg128_row": nc.dram_tensor("c_neg128_row", [1, 512], BF16, kind="ExternalInput").ap(),
        "eps": nc.dram_tensor("c_eps", [128, 4], F32, kind="ExternalInput").ap(),
    }



def emit_ln_tile(p, xt, stats, xn_out, sq_scrap, eps_ap=None, gain_col=None,
                 rms2_eps=None, rms2_bias=None, rms_only=False):
    v, s = p.e['v'], p.e['s']
    lnx, lny = p.sem("lnx"), p.sem("lny")
    W = xt[:].shape[1]
    vsum = stats[:, 0:1]; vsq = stats[:, 1:2]; vm = stats[:, 2:3]
    vneg = stats[:, 3:4]; vsd = stats[:, 4:5]; vrstd = stats[:, 5:6]
    if not rms_only:
        p.vsync(v.reduce_sum(vsum, xt[:], axis=AX.X))
    lnx.inc(s.activation(sq_scrap[:], xt[:], AF.Square), 1)
    v.wait_ge(lnx.h, lnx.v)
    p.vsync(v.reduce_sum(vsq, sq_scrap[:], axis=AX.X))
    if rms_only:
        p.vsync(v.memset(vm, 0.0))
        p.vsync(v.memset(vsum, 0.0))
    else:
        p.vsync(v.tensor_scalar_mul(vm, vsum, 1.0 / W))
    p.vsync(v.tensor_scalar_mul(vneg, vsq, 1.0 / W))
    lny.inc(v.scalar_tensor_tensor(vneg, vm, vm, vneg, ALU.mult, ALU.subtract), 1)
    v.wait_ge(lny.h, lny.v)
    s.wait_ge(lny.h, lny.v)
    lnx.inc(s.activation(vsd, vneg, AF.Sqrt, bias=eps_ap, scale=-1.0), 1)
    v.wait_ge(lnx.h, lnx.v)
    p.vsync(v.reciprocal(vrstd, vsd))
    scale_col = vrstd
    if rms2_eps is not None:
        v6 = stats[:, 6:7]; v7 = stats[:, 7:8]
        p.vsync(v.tensor_mul(v6, vrstd, vrstd))
        lny.inc(v.scalar_tensor_tensor(v7, vneg, -1.0, v6, ALU.mult, ALU.mult), 1)
        v.wait_ge(lny.h, lny.v)
        s.wait_ge(lny.h, lny.v)
        lnx.inc(s.activation(vsd, v7, AF.Sqrt, bias=rms2_bias), 1)
        v.wait_ge(lnx.h, lnx.v)
        p.vsync(v.reciprocal(v6, vsd))
        p.vsync(v.tensor_mul(v7, vrstd, v6))
        scale_col = v7
    if gain_col is not None:
        p.vsync(v.tensor_mul(scale_col, scale_col, gain_col))
    return v.tensor_scalar(xn_out[:], xt[:], vm, scale_col, ALU.subtract, ALU.mult)


def emit_ln_to_xnT(p, x_hbm, xnT, ident, eps_ap, ntt=NTT, xn_tm=None,
                   gainT_c=None, rms2_eps=None, rms2_bias=None, tag="ln",
                   rms_only=False, x_store=None):
    nc = p.nc
    y, t, v, s = p.e['y'], p.e['t'], p.e['v'], p.e['s']
    ld = p.sem("ld"); mm = p.sem("mm"); ev = p.sem("ev")
    T = ntt * 128
    sc = ScopeAlloc(nc)
    xtile = sc.sb(tag + "_xtile", [128, DM], F32)
    scrap = sc.sb(tag + "_scrap", [128, DM], BF16)
    xn_tile = sc.sb(tag + "_xn", [128, DM], BF16)
    stats = sc.sb(tag + "_stats", [128, 8], F32)
    ps_tr = [sc.ps(tag + "_ps_tr0", [128, 128], BF16), sc.ps(tag + "_ps_tr1", [128, 128], BF16)]
    lnx = p.sem("lnx")
    for tt in range(ntt):
        y.wait_ge(ev.h, ev.v)
        y.wait_ge(lnx.h, lnx.v)
        ld.inc(y.dma_start(out=xtile[:], in_=x_hbm[tt * 128:(tt + 1) * 128, :]), 16)
        v.wait_ge(ld.h, ld.v)
        s.wait_ge(ld.h, ld.v)
        v.wait_ge(mm.h, mm.v)
        last = emit_ln_tile(p, xtile, stats, xn_tile, scrap, eps_ap=eps_ap,
                            rms2_eps=rms2_eps, rms2_bias=rms2_bias, rms_only=rms_only)
        ev.inc(last, 1)
        if xn_tm is not None:
            v.tensor_copy(xn_tm[tt][:], xn_tile[:])
        t.wait_ge(ev.h, ev.v)
        tr_ring = Ring(p, ps_tr, "trr")
        for j in range(DCH):
            bank = tr_ring.produce(t)
            tr_ring.mm_done(t.transpose(bank[:], xn_tile[:, j * 128:(j + 1) * 128], ident[:]))
            bb = tr_ring.consume_wait(v)
            dst = xnT[:, j * T + tt * 128: j * T + (tt + 1) * 128]
            if gainT_c is None:
                tr_ring.ev_done(v.tensor_copy(dst, bb[:]))
            else:
                tr_ring.ev_done(v.tensor_scalar_mul(dst, bb[:], gainT_c[:, j:j + 1]))
        mm.inc(t.nop(), 1)
    sc.close()
    p.barrier()



def build_l1_ffn():
    p = Prog()
    nc = p.nc
    HID = 5632
    HCH = HID // 128
    NSUP = HID // 512

    x = nc.dram_tensor("x", [TOK, DM], F32, kind="ExternalInput").ap()
    wgT = nc.dram_tensor("wgT", [DM, HID], I32, kind="ExternalInput").ap()
    wuT = nc.dram_tensor("wuT", [DM, HID], I32, kind="ExternalInput").ap()
    wdT = nc.dram_tensor("wdT", [HID, DM], I32, kind="ExternalInput").ap()
    sg_c = nc.dram_tensor("sg_c", [128, HCH], F32, kind="ExternalInput").ap()
    su_c = nc.dram_tensor("su_c", [128, HCH], F32, kind="ExternalInput").ap()
    sd_flat = nc.dram_tensor("sd_flat", [1, DM], F32, kind="ExternalInput").ap()
    xout = nc.dram_tensor("xout", [TOK, DM], F32, kind="ExternalOutput").ap()
    hid_scr = nc.dram_tensor("hid_scr", [NTT, 128, HCH * 128], BF16)
    cst = decl_consts(nc)

    ident = nc.alloc_sbuf_tensor("ident", [128, 128], BF16)
    ones_col = nc.alloc_sbuf_tensor("ones_col", [128, 1], BF16)
    ones_row = nc.alloc_sbuf_tensor("ones_row", [1, 128], BF16)
    neg128_row = nc.alloc_sbuf_tensor("neg128_row", [1, 512], BF16)
    sgc = nc.alloc_sbuf_tensor("sgc", [128, HCH], F32)
    suc = nc.alloc_sbuf_tensor("suc", [128, HCH], F32)
    sd_rows = nc.alloc_sbuf_tensor("sd_rows", [1, DM], BF16)
    xnrs = nc.alloc_sbuf_tensor("xnrs", [1, TOK], BF16)
    hidrs = nc.alloc_sbuf_tensor("hidrs", [1, TOK], BF16)
    rs_f32 = nc.alloc_sbuf_tensor("rs_f32", [1, TOK], F32)
    sdrow = nc.alloc_sbuf_tensor("sdrow", [128, DM], F32)
    epsb = nc.alloc_sbuf_tensor("epsb", [128, 4], F32)

    ld = p.sem("ld"); wl = p.sem("wl"); mm = p.sem("mm"); ev = p.sem("ev")
    sty = p.sem("sty"); stg = p.sem("stg"); hl = p.sem("hl"); dmg = p.sem("dmg")

    if True:
        y, t, v, s, g = (p.e[k] for k in ('y', 't', 'v', 's', 'g'))

        ld.inc(y.dma_start(out=ident[:], in_=cst["ident"]), 16)
        ld.inc(y.dma_start(out=ones_col[:], in_=cst["ones_col"]), 16)
        ld.inc(y.dma_start(out=ones_row[:], in_=cst["ones_row"]), 16)
        ld.inc(y.dma_start(out=neg128_row[:], in_=cst["neg128_row"]), 16)
        ld.inc(y.dma_start(out=sgc[:], in_=sg_c), 16)
        ld.inc(y.dma_start(out=suc[:], in_=su_c), 16)
        dmg.inc(g.dma_start(out=sd_rows[:], in_=sd_flat), 16)
        ld.inc(y.dma_start(out=epsb[:], in_=cst["eps"]), 16)
        stg.inc(g.dma_start(out=xout, in_=x), 16)
        for e in (y, t, v, s, g):
            e.wait_ge(ld.h, ld.v)
            e.wait_ge(dmg.h, dmg.v)

        sc_a = ScopeAlloc(nc)
        xnT = sc_a.sb("xnT", [128, DCH * TOK], BF16)
        emit_ln_to_xnT(p, x, xnT, ident, epsb[:, 0:1])

        with nc.psum_tensor("ps_rs", [1, 512], F32) as ps_rs, \
             nc.psum_tensor("ps_sd", [128, 512], F32) as ps_sd:
            for tb in range(4):
                pairs = [(ones_col[:], xnT[:, k * TOK + tb * 512:k * TOK + (tb + 1) * 512])
                         for k in range(DCH)]
                mm.inc(mm_accum(t, ps_rs[:], pairs), 1)
                v.wait_ge(mm.h, mm.v)
                ev.inc(v.tensor_copy(rs_f32[:, tb * 512:(tb + 1) * 512], ps_rs[:]), 1)
                t.wait_ge(ev.h, ev.v)
                mm.inc(t.matmul(ps_sd[:], ones_row[:], sd_rows[:, tb * 512:(tb + 1) * 512],
                                start=True, stop=True), 1)
                v.wait_ge(mm.h, mm.v)
                ev.inc(v.tensor_copy(sdrow[:, tb * 512:(tb + 1) * 512], ps_sd[:]), 1)
                t.wait_ge(ev.h, ev.v)
            v.tensor_copy(xnrs[:], rs_f32[:])
        p.barrier()

        sc_b = ScopeAlloc(nc)
        wbuf = [sc_b.sb(f"wbuf{i}", [128, DCH * 1024], BF16) for i in range(2)]
        gu_tile = sc_b.sb("gu_tile", [128, 2 * TOK], F32)
        hid_tile = sc_b.sb("hid_tile", [128, TOK], BF16)
        with nc.psum_tensor("ps_g0", [128, 512], F32) as pg0, \
             nc.psum_tensor("ps_g1", [128, 512], F32) as pg1, \
             nc.psum_tensor("ps_u0", [128, 512], F32) as pu0, \
             nc.psum_tensor("ps_u1", [128, 512], F32) as pu1, \
             nc.psum_tensor("ps_h0", [1, 512], F32) as ph0, \
             nc.psum_tensor("ps_h1", [1, 512], F32) as ph1, \
             nc.psum_tensor("ps_h2", [1, 512], F32) as ph2, \
             nc.psum_tensor("ps_h3", [1, 512], F32) as ph3:
            ps_hrs = [ph0, ph1, ph2, ph3]
            g_ring = Ring(p, [pg0, pg1], "gr")
            u_ring = Ring(p, [pu0, pu1], "ur")
            for sup in range(NSUP):
                wb = wbuf[sup % 2]
                if sup >= 2:
                    g.wait_ge(u_ring.mm.h, u_ring.mm_base + 16 * (sup - 1))
                for k in range(DCH):
                    wl.inc(g.dma_start(out=wb[:, k * 1024:k * 1024 + 512],
                                       in_=wgT[k * 128:(k + 1) * 128, sup * 512:(sup + 1) * 512]), 16)
                    wl.inc(g.dma_start(out=wb[:, k * 1024 + 512:(k + 1) * 1024],
                                       in_=wuT[k * 128:(k + 1) * 128, sup * 512:(sup + 1) * 512]), 16)
                t.wait_ge(wl.h, wl.v)
                for hc4 in range(4):
                    hc = sup * 4 + hc4
                    for tb in range(4):
                        xn_blk = [xnT[:, k * TOK + tb * 512:k * TOK + (tb + 1) * 512]
                                  for k in range(DCH)]
                        bank = g_ring.produce(t)
                        pairs = [(wb[:, k * 1024 + hc4 * 128:k * 1024 + (hc4 + 1) * 128],
                                  xn_blk[k]) for k in range(DCH)]
                        pairs.append((neg128_row[:, 0:128], xnrs[:, tb * 512:(tb + 1) * 512]))
                        g_ring.mm_done(mm_accum(t, bank[:], pairs))
                        bank_u = u_ring.produce(t)
                        pairs = [(wb[:, k * 1024 + 512 + hc4 * 128:k * 1024 + 512 + (hc4 + 1) * 128],
                                  xn_blk[k]) for k in range(DCH)]
                        pairs.append((neg128_row[:, 0:128], xnrs[:, tb * 512:(tb + 1) * 512]))
                        u_ring.mm_done(mm_accum(t, bank_u[:], pairs))
                        s.wait_ge(ev.h, ev.v)
                        bg = g_ring.consume_wait(s)
                        g_ring.ev_done(s.activation(gu_tile[:, tb * 512:(tb + 1) * 512],
                                                    bg[:], AF.Silu, scale=sgc[:, hc:hc + 1]))
                        bu = u_ring.consume_wait(v)
                        u_ring.ev_done(v.tensor_scalar_mul(
                            gu_tile[:, TOK + tb * 512:TOK + (tb + 1) * 512],
                            bu[:], suc[:, hc:hc + 1]))
                        v.wait_ge(g_ring.ev.h, g_ring.ev.v)
                        v.wait_ge(sty.h, sty.v)
                        v.wait_ge(mm.h, mm.v)
                        htt = v.tensor_mul(hid_tile[:, tb * 512:(tb + 1) * 512],
                                           gu_tile[:, tb * 512:(tb + 1) * 512],
                                           gu_tile[:, TOK + tb * 512:TOK + (tb + 1) * 512])
                        ev.inc(htt, 1)
                        t.wait_ge(ev.h, ev.v)
                        mm.inc(t.matmul(ps_hrs[tb][:], ones_col[:],
                                        hid_tile[:, tb * 512:(tb + 1) * 512],
                                        start=(hc == 0), stop=(hc == HCH - 1),
                                        skip_group_check=True), 1)
                        y.wait_ge(mm.h, mm.v)
                        sty.inc(y.dma_start(
                            out=hid_scr[tb * 4:(tb + 1) * 4, :, hc * 128:(hc + 1) * 128]
                                .rearrange("n p q -> p n q"),
                            in_=hid_tile[:, tb * 512:(tb + 1) * 512]
                                .rearrange("p (n q) -> p n q", q=128)), 16)
            for tb in range(4):
                v.wait_ge(mm.h, mm.v)
                ev.inc(v.tensor_copy(rs_f32[:, tb * 512:(tb + 1) * 512], ps_hrs[tb][:]), 1)
            v.wait_ge(ev.h, ev.v)
            ev.inc(v.tensor_copy(hidrs[:], rs_f32[:]), 1)
        sc_b.close()
        sc_a.close()
        p.barrier()

        sc_c = ScopeAlloc(nc)
        dwbuf = [sc_c.sb(f"dwbuf{i}", [128, HCH * 512], BF16) for i in range(2)]
        hbuf = [sc_c.sb(f"hbuf{i}", [128, HCH * 128], BF16) for i in range(2)]
        out_tile = sc_c.sb("out_tile", [128, 512], F32)
        g.wait_ge(stg.h, stg.v)
        g.wait_ge(sty.h, sty.v)
        with nc.psum_tensor("ps_d0", [128, 512], F32) as pd0, \
             nc.psum_tensor("ps_d1", [128, 512], F32) as pd1:
            d_ring = Ring(p, [pd0, pd1], "dr")
            for db in range(4):
                dwb = dwbuf[db % 2]
                if db >= 2:
                    g.wait_ge(d_ring.mm.h, d_ring.mm_base + 16 * (db - 1))
                for hc in range(HCH):
                    wl.inc(g.dma_start(out=dwb[:, hc * 512:(hc + 1) * 512],
                                       in_=wdT[hc * 128:(hc + 1) * 128, db * 512:(db + 1) * 512]), 16)
                for tt in range(NTT):
                    hb = hbuf[tt % 2]
                    gidx = db * 16 + tt
                    if gidx >= 2:
                        y.wait_ge(d_ring.mm.h, d_ring.mm_base + gidx - 1)
                    hl.inc(y.dma_start(out=hb[:], in_=hid_scr[tt]), 16)
                    t.wait_ge(wl.h, wl.v)
                    t.wait_ge(hl.h, hl.v)
                    bank = d_ring.produce(t)
                    pairs = [(hb[:, hc * 128:(hc + 1) * 128], dwb[:, hc * 512:(hc + 1) * 512])
                             for hc in range(HCH)]
                    pairs.append((hidrs[:, tt * 128:(tt + 1) * 128], neg128_row[:]))
                    d_ring.mm_done(mm_accum(t, bank[:], pairs))
                    v.wait_ge(stg.h, stg.v)
                    bd = d_ring.consume_wait(v)
                    d_ring.ev_done(v.tensor_mul(out_tile[:], bd[:],
                                                sdrow[:, db * 512:(db + 1) * 512]))
                    g.wait_ge(d_ring.ev.h, d_ring.ev.v)
                    stg.inc(g.dma_start(out=xout[tt * 128:(tt + 1) * 128, db * 512:(db + 1) * 512],
                                       in_=out_tile[:], accum_op=ALU.add), 16)
        g.wait_ge(stg.h, stg.v)
        sc_c.close()
        p.barrier()
    p.compile_block()
    return p




def build_l0_ffn(upto=9, debug=False):
    p = Prog()
    nc = p.nc
    TH = 8192
    TCH = 64
    NSUP = TH // 512

    x = nc.dram_tensor("x", [TOK, DM], F32, kind="ExternalInput").ap()
    upT = nc.dram_tensor("upT", [DM, TH], F32, kind="ExternalInput").ap()
    dnT = nc.dram_tensor("dnT", [TH, DM], F32, kind="ExternalInput").ap()
    upsc_c = nc.dram_tensor("upsc_c", [128, TCH], F32, kind="ExternalInput").ap()
    dsc = nc.dram_tensor("dsc", [64, DM], F32, kind="ExternalInput").ap()
    osc = nc.dram_tensor("osc", [64, 1], F32, kind="ExternalInput").ap()
    normw_c = nc.dram_tensor("normw_c", [128, DCH], F32, kind="ExternalInput").ap()
    xout = nc.dram_tensor("xout", [TOK, DM], F32, kind="ExternalOutput").ap()
    kd = dict(kind="ExternalOutput") if debug else {}
    if debug:
        dbg_wb = nc.dram_tensor("dbg_wb", [128, DCH * 512], BF16, kind="ExternalOutput").ap()
        dbg_hid = nc.dram_tensor("dbg_hid", [128, 512], BF16, kind="ExternalOutput").ap()
        dbg_xnt = nc.dram_tensor("dbg_xnt", [128, 512], BF16, kind="ExternalOutput").ap()
        dbg_rt = nc.dram_tensor("dbg_rt", [128, 96], F32, kind="ExternalOutput").ap()
        dbg_sig = nc.dram_tensor("dbg_sig", [128, DCH * 64], BF16, kind="ExternalOutput").ap()
    hid_scr = nc.dram_tensor("hid_scr", [NTT, 128, TH], BF16, **kd)
    hid2_scr = nc.dram_tensor("hid2_scr", [NTT, 128, TH], BF16, **kd)
    dsel_scr = nc.dram_tensor("dsel_scr", [TOK, DM], BF16, **kd)
    oh_scr = nc.dram_tensor("oh_scr", [64, TOK], BF16, **kd)
    cst = decl_consts(nc)

    ident = nc.alloc_sbuf_tensor("ident", [128, 128], BF16)
    ones_row = nc.alloc_sbuf_tensor("ones_row", [1, 128], BF16)
    epsb = nc.alloc_sbuf_tensor("epsb", [128, 4], F32)
    upsc = nc.alloc_sbuf_tensor("upsc", [128, TCH], F32)
    nwc = nc.alloc_sbuf_tensor("nwc", [128, DCH], F32)
    tsig_sum = nc.alloc_sbuf_tensor("tsig_sum", [128, DCH * 64], F32)
    tsig_fm = nc.alloc_sbuf_tensor("tsig_fm", [128, DCH * 64], BF16)
    csig_fm = nc.alloc_sbuf_tensor("csig_fm", [128, DCH * 8], BF16)
    sds = nc.alloc_sbuf_tensor("sds", [64, DM], BF16)
    sds_f = nc.alloc_sbuf_tensor("sds_f", [64, DM], F32)

    ld = p.sem("ld"); wl = p.sem("wl"); mm = p.sem("mm"); ev = p.sem("ev")
    sty = p.sem("sty"); stg = p.sem("stg"); hl = p.sem("hl")
    y, t, v, s, g = (p.e[k] for k in ('y', 't', 'v', 's', 'g'))

    ld.inc(y.dma_start(out=ident[:], in_=cst["ident"]), 16)
    ld.inc(y.dma_start(out=ones_row[:], in_=cst["ones_row"]), 16)
    ld.inc(y.dma_start(out=epsb[:], in_=cst["eps"]), 16)
    ld.inc(y.dma_start(out=upsc[:], in_=upsc_c), 16)
    ld.inc(y.dma_start(out=nwc[:], in_=normw_c), 16)
    ld.inc(y.dma_start(out=sds_f[:], in_=dsc), 16)
    stg.inc(g.dma_start(out=xout, in_=x), 16)
    for e in (y, t, v, s, g):
        e.wait_ge(ld.h, ld.v)
    oscb = nc.alloc_sbuf_tensor("oscb", [64, 1], F32)
    ld.inc(y.dma_start(out=oscb[:], in_=osc), 16)
    v.wait_ge(ld.h, ld.v)
    ev.inc(v.tensor_scalar_mul(sds[:], sds_f[:], oscb[:]), 1)

    sc_a = ScopeAlloc(nc)
    xnT = sc_a.sb("xnT", [128, DCH * TOK], BF16)
    if upto >= 1:
        emit_ln_to_xnT(p, x, xnT, ident, epsb[:, 0:1], gainT_c=nwc,
                       rms2_eps=1e-6, rms2_bias=epsb[:, 1:2])
    if upto < 2:
        g.wait_ge(stg.h, stg.v)
        p.compile_block()
        return p

    sc_b = ScopeAlloc(nc)
    wbuf = [sc_b.sb(f"wbuf{i}", [128, DCH * 512], BF16) for i in range(2)]
    wraw = [sc_b.sb(f"wraw{i}", [128, 512], F32) for i in range(2)]
    hid_bf = sc_b.sb("hid_bf", [128, TOK], BF16)
    ps_u = [sc_b.ps("ps_u0", [128, 512]), sc_b.ps("ps_u1", [128, 512])]
    u_ring = Ring(p, ps_u, "l0u")
    sgn = p.sem("sgn")
    sgn_base = sgn.v
    for sup in range(NSUP):
        wb = wbuf[sup % 2]
        if sup >= 2:
            t.wait_ge(u_ring.mm.h, u_ring.mm_base + 16 * (sup - 1))
        for k in range(DCH):
            idx = sup * DCH + k
            if idx >= 2:
                y.wait_ge(sgn.h, sgn_base + idx - 1)
            wl.inc(y.dma_start(out=wraw[idx % 2][:], in_=upT[k * 128:(k + 1) * 128,
                                                    sup * 512:(sup + 1) * 512]), 16)
            s.wait_ge(wl.h, wl.v)
            if sup >= 2:
                s.wait_ge(u_ring.mm.h, u_ring.mm_base + 16 * (sup - 1))
            sgn.inc(s.activation(wb[:, k * 512:(k + 1) * 512], wraw[idx % 2][:], AF.Sign), 1)
            v.wait_ge(sgn.h, sgn.v)
            ev.inc(v.reduce_sum(
                tsig_sum[:, k * 64 + sup * 4: k * 64 + sup * 4 + 4]
                    .rearrange("p (a o) -> p a o", o=1),
                wb[:, k * 512:(k + 1) * 512].rearrange("p (a q) -> p a q", q=128),
                axis=AX.X), 1)
        t.wait_ge(sgn.h, sgn.v)
        t.wait_ge(ev.h, ev.v)
        for hc4 in range(4):
            hc = sup * 4 + hc4
            for tb in range(4):
                bank = u_ring.produce(t)
                pairs = [(wb[:, k * 512 + hc4 * 128:k * 512 + (hc4 + 1) * 128],
                          xnT[:, k * TOK + tb * 512:k * TOK + (tb + 1) * 512])
                         for k in range(DCH)]
                u_ring.mm_done(mm_accum(t, bank[:], pairs))
                bu = u_ring.consume_wait(s)
                s.wait_ge(sty.h, sty.v)
                act = s.activation(hid_bf[:, tb * 512:(tb + 1) * 512],
                                   bu[:], AF.Relu, scale=upsc[:, hc:hc + 1])
                u_ring.ev_done(act)
                y.wait_ge(u_ring.ev.h, u_ring.ev.v)
                if debug and sup == 0 and hc4 == 0 and tb == 0:
                    sty.inc(y.dma_start(out=dbg_wb, in_=wb[:]), 16)
                    sty.inc(y.dma_start(out=dbg_hid, in_=hid_bf[:, 0:512]), 16)
                    sty.inc(y.dma_start(out=dbg_xnt, in_=xnT[:, 0:512]), 16)
                sty.inc(y.dma_start(
                    out=hid_scr[tb * 4:(tb + 1) * 4, :, hc * 128:(hc + 1) * 128]
                        .rearrange("n p q -> p n q"),
                    in_=hid_bf[:, tb * 512:(tb + 1) * 512]
                        .rearrange("p (n q) -> p n q", q=128)), 16)
    if upto < 3:
        g.wait_ge(stg.h, stg.v)
        y.wait_ge(sty.h, sty.v)
        p.compile_block()
        return p
    lnx = p.sem("lnx")
    s.wait_ge(ev.h, ev.v)
    lnx.inc(s.activation(tsig_fm[:], tsig_sum[:], AF.Sign), 1)
    v.wait_ge(lnx.h, lnx.v)
    ev.inc(v.reduce_sum(
        csig_fm[:].rearrange("p (k c o) -> p k c o", k=DCH, o=1).bitcast(BF16) if False else
        tsig_sum[:, 0:DCH * 8].rearrange("p (k c o) -> p k c o", k=DCH, o=1),
        tsig_fm[:].rearrange("p (k c j) -> p k c j", k=DCH, j=8), axis=AX.X), 1)
    s.wait_ge(ev.h, ev.v)
    lnx.inc(s.activation(csig_fm[:], tsig_sum[:, 0:DCH * 8], AF.Sign), 1)
    sc_b.close()
    p.barrier()

    sc_r = ScopeAlloc(nc)
    scf = sc_r.sb("scf", [128, 64], F32)
    csf = sc_r.sb("csf", [128, 8], F32)
    rmax = sc_r.sb("rmax", [128, 1], F32)
    eqm = sc_r.sb("eqm", [128, 8], F32)
    cum = sc_r.sb("cum", [128, 8], F32)
    cmask = sc_r.sb("cmask", [128, 8], F32)
    lmask = sc_r.sb("lmask", [128, 8], F32)
    lsc = [sc_r.sb(f"lsc{i}", [128, 8], F32) for i in range(2)]
    oh = sc_r.sb("oh", [128, 128], BF16)
    ohT = sc_r.sb("ohT", [128, 128], BF16)
    dsel_bf = sc_r.sb("dsel_bf", [128, DM], BF16)
    ps_s = sc_r.ps("ps_s", [128, 64])
    ps_c = sc_r.ps("ps_c", [128, 8])
    ps_o = sc_r.ps("ps_o", [128, 128], BF16)
    ps_dsel = sc_r.ps("ps_dsel", [128, 512])
    if debug:
        dbg_rt_sb = sc_r.sb("dbg_rt_sb", [128, 96], F32)
        ev.inc(v.tensor_copy(dbg_sig_sb_dummy := tsig_fm[:], tsig_fm[:]), 1) if False else None
        y.wait_ge(ev.h, ev.v)
        sty.inc(y.dma_start(out=dbg_sig, in_=tsig_fm[:]), 16)
    ev.inc(v.memset(oh[:], 0.0), 1)

    def argmax_mask(src, n, outmask):
        p.vsync(v.reduce_max(rmax[:], src[:, 0:n], axis=AX.X))
        p.vsync(v.tensor_tensor(eqm[:, 0:n], src[:, 0:n],
                        rmax[:].to_broadcast((128, n)), ALU.is_equal))
        for j in range(n):
            if j == 0:
                p.vsync(v.tensor_copy(outmask[:, 0:1], eqm[:, 0:1]))
                p.vsync(v.tensor_copy(cum[:, 0:1], eqm[:, 0:1]))
            else:
                p.vsync(v.tensor_scalar(outmask[:, j:j + 1], cum[:, j - 1:j], -1.0, 1.0,
                                ALU.mult, ALU.add))
                p.vsync(v.tensor_mul(outmask[:, j:j + 1], outmask[:, j:j + 1], eqm[:, j:j + 1]))
                p.vsync(v.tensor_add(cum[:, j:j + 1], cum[:, j - 1:j], outmask[:, j:j + 1]))

    for tt in range(NTT):
        xn_t = [xnT[:, k * TOK + tt * 128:k * TOK + (tt + 1) * 128] for k in range(DCH)]
        mm.inc(mm_accum(t, ps_s[:], [(xn_t[k], tsig_fm[:, k * 64:(k + 1) * 64])
                                     for k in range(DCH)]), 1)
        mm.inc(mm_accum(t, ps_c[:], [(xn_t[k], csig_fm[:, k * 8:(k + 1) * 8])
                                     for k in range(DCH)]), 1)
        v.wait_ge(mm.h, mm.v)
        p.vsync(v.tensor_copy(scf[:], ps_s[:]))
        ev.inc(p.vsync(v.tensor_copy(csf[:], ps_c[:])), 1) if False else None
        cp2 = v.tensor_copy(csf[:], ps_c[:])
        ev.inc(cp2, 1)
        v.wait_ge(ev.h, ev.v)
        t.wait_ge(ev.h, ev.v)
        argmax_mask(csf, 8, cmask)
        p.vsync(v.tensor_scalar_mul(lsc[0][:], scf[:, 0:8], cmask[:, 0:1]))
        for c in range(1, 8):
            p.vsync(v.scalar_tensor_tensor(lsc[c % 2][:], scf[:, c * 8:(c + 1) * 8],
                                   cmask[:, c:c + 1], lsc[(c - 1) % 2][:],
                                   ALU.mult, ALU.add))
        argmax_mask(lsc[7 % 2], 8, lmask)
        for c in range(8):
            p.vsync(v.tensor_scalar_mul(oh[:, c * 8:(c + 1) * 8], lmask[:], cmask[:, c:c + 1]))
        if debug and tt == 0:
            v.tensor_copy(dbg_rt_sb[:, 0:8], csf[:])
            v.tensor_copy(dbg_rt_sb[:, 8:16], cmask[:])
            v.tensor_copy(dbg_rt_sb[:, 16:24], lmask[:])
            ev.inc(v.tensor_copy(dbg_rt_sb[:, 32:96], scf[:]), 1)
            y.wait_ge(ev.h, ev.v)
            sty.inc(y.dma_start(out=dbg_rt, in_=dbg_rt_sb[:]), 16)
        ev.inc(v.engine_nop(), 1)
        t.wait_ge(ev.h, ev.v)
        mm.inc(t.transpose(ps_o[:], oh[:], ident[:]), 1)
        v.wait_ge(mm.h, mm.v)
        ev.inc(v.tensor_copy(ohT[:], ps_o[:]), 1)
        y.wait_ge(ev.h, ev.v)
        sty.inc(y.dma_start(out=oh_scr[:, tt * 128:(tt + 1) * 128], in_=ohT[0:64, :]), 16)
        t.wait_ge(ev.h, ev.v)
        for db in range(4):
            mm.inc(t.matmul(ps_dsel[:], ohT[0:64, :], sds[:, db * 512:(db + 1) * 512],
                            start=True, stop=True), 1)
            v.wait_ge(mm.h, mm.v)
            ev.inc(v.tensor_copy(dsel_bf[:, db * 512:(db + 1) * 512], ps_dsel[:]), 1)
            t.wait_ge(ev.h, ev.v)
        y.wait_ge(ev.h, ev.v)
        sty.inc(y.dma_start(out=dsel_scr[tt * 128:(tt + 1) * 128, :], in_=dsel_bf[:]), 16)
        v.wait_ge(sty.h, sty.v)
    sc_r.close()
    sc_a.close()
    p.barrier()

    if upto < 4:
        g.wait_ge(stg.h, stg.v)
        y.wait_ge(sty.h, sty.v)
        p.compile_block()
        return p
    sc_m = ScopeAlloc(nc)
    ohrow_t = sc_m.sb("ohrow_t", [1, 64 * 128], BF16)
    hbm = [sc_m.sb(f"hbm{i}", [128, TH], BF16) for i in range(2)]
    mk = sc_m.sb("mk", [128, TH], BF16)
    ps_mb = [sc_m.ps(f"ps_mb{i}", [128, 512]) for i in range(2)]
    hl2 = p.sem("hl2")
    mb_ring = Ring(p, ps_mb, "mbr")
    for tt in range(NTT):
        hb = hbm[tt % 2]
        if tt >= 2:
            y.wait_ge(ev.h, ev.v)
        if tt > 0:
            y.wait_ge(mb_ring.mm.h, mb_ring.mm.v)
        hl2.inc(y.dma_start(out=ohrow_t[:].rearrange("o (a b) -> o a b", a=64),
                            in_=oh_scr[:, tt * 128:(tt + 1) * 128].unsqueeze(0)), 16)
        hl2.inc(y.dma_start(out=hb[:], in_=hid_scr[tt]), 16)
        for qb in range(TH // 512):
            bank = mb_ring.produce(t)
            t.wait_ge(hl2.h, hl2.v)
            mb_ring.mm_done(t.matmul(bank[:], ones_row[:],
                                     ohrow_t[:, qb * 512:(qb + 1) * 512],
                                     start=True, stop=True))
            bb = mb_ring.consume_wait(v)
            mb_ring.ev_done(v.tensor_mul(mk[:, qb * 512:(qb + 1) * 512],
                                         hb[:, qb * 512:(qb + 1) * 512], bb[:]))
        ev.inc(v.engine_nop(), 1)
        y.wait_ge(ev.h, ev.v)
        sty.inc(y.dma_start(out=hid2_scr[tt], in_=mk[:]), 16)
        y.wait_ge(sty.h, sty.v)
    sc_m.close()
    p.barrier()

    if upto < 5:
        g.wait_ge(stg.h, stg.v)
        y.wait_ge(sty.h, sty.v)
        p.compile_block()
        return p
    sc_c = ScopeAlloc(nc)
    dwbuf = [sc_c.sb(f"dwbuf{i}", [128, TCH * 512], BF16) for i in range(2)]
    draw = sc_c.sb("draw", [128, 512], F32)
    hbuf = [sc_c.sb(f"hbuf{i}", [128, TH], BF16) for i in range(2)]
    dselt = [sc_c.sb(f"dselt{i}", [128, 512], BF16) for i in range(2)]
    out_tile = sc_c.sb("out_tile", [128, 512], F32)
    g.wait_ge(stg.h, stg.v)
    g.wait_ge(sty.h, sty.v)
    ps_d = [sc_c.ps("ps_d0", [128, 512]), sc_c.ps("ps_d1", [128, 512])]
    d_ring = Ring(p, ps_d, "l0dr")
    sgn2 = p.sem("sgn2")
    for db in range(4):
        dwb = dwbuf[db % 2]
        if db >= 2:
            t.wait_ge(d_ring.mm.h, d_ring.mm_base + 16 * (db - 1))
        for hc in range(TCH):
            y.wait_ge(sgn2.h, sgn2.v)
            wl.inc(y.dma_start(out=draw[:], in_=dnT[hc * 128:(hc + 1) * 128,
                                                    db * 512:(db + 1) * 512]), 16)
            s.wait_ge(wl.h, wl.v)
            if db >= 2:
                s.wait_ge(d_ring.mm.h, d_ring.mm_base + 16 * (db - 1))
            sgn2.inc(s.activation(dwb[:, hc * 512:(hc + 1) * 512], draw[:], AF.Sign), 1)
        t.wait_ge(sgn2.h, sgn2.v)
        for tt in range(NTT):
            hb = hbuf[tt % 2]
            gidx = db * 16 + tt
            if gidx >= 2:
                y.wait_ge(d_ring.mm.h, d_ring.mm_base + gidx - 1)
            hl.inc(y.dma_start(out=hb[:], in_=hid2_scr[tt]), 16)
            if gidx >= 2:
                y.wait_ge(d_ring.ev.h, d_ring.ev_base + gidx - 1)
            hl.inc(y.dma_start(out=dselt[gidx % 2][:], in_=dsel_scr[tt * 128:(tt + 1) * 128,
                                                          db * 512:(db + 1) * 512]), 16)
            t.wait_ge(hl.h, hl.v)
            bank = d_ring.produce(t)
            pairs = [(hb[:, hc * 128:(hc + 1) * 128], dwb[:, hc * 512:(hc + 1) * 512])
                     for hc in range(TCH)]
            d_ring.mm_done(mm_accum(t, bank[:], pairs))
            v.wait_ge(stg.h, stg.v)
            bd = d_ring.consume_wait(v)
            d_ring.ev_done(v.tensor_mul(out_tile[:], bd[:], dselt[gidx % 2][:]))
            g.wait_ge(d_ring.ev.h, d_ring.ev.v)
            stg.inc(g.dma_start(out=xout[tt * 128:(tt + 1) * 128, db * 512:(db + 1) * 512],
                               in_=out_tile[:], accum_op=ALU.add), 16)
    g.wait_ge(stg.h, stg.v)
    sc_c.close()
    p.barrier()
    p.compile_block()
    return p


def prep_l0_attn_a(inputs, core):
    base = dict(host_consts())
    for nm in ("Wq", "Wk", "Wv", "Wg", "Wp"):
        base[nm + "T"] = np.ascontiguousarray(np.asarray(inputs["L0_attn_a_" + nm], np.float32).T)
    h = core % 2
    cm = np.zeros((NTT, 8, 128, 512), dtype=BF16NP)
    for rt in range(NTT):
        grow = h * 2048 + rt * 128 + np.arange(128)[:, None]
        ncb = (16 + rt + 1 + 3) // 4
        for cb in range(8):
            gcol = cb * 512 + np.arange(512)[None, :]
            cm[rt, cb] = (gcol <= grow).astype(BF16NP)
    base["cmask"] = cm
    return base


def run_l0_attn_a(x_full, inputs, prog=None):
    if prog is None:
        prog = build_l0_attn_a()
    shards = shard_phaseA(x_full)
    in_maps = []
    for c in range(NCORES):
        m = prep_l0_attn_a(inputs, c)
        m["x"] = shards[c]
        in_maps.append(m)
    res = run_bass_kernel_spmd(prog.nc, in_maps, core_ids=list(range(NCORES)))
    return unshard_phaseA([res.results[c]["xout"] for c in range(NCORES)])


def prep_l0_attn_b(inputs, core):
    base = dict(host_consts())
    base["wqkv"] = np.asarray(inputs["L0_attn_b_wqkv"], np.float32)
    base["anw_c"] = _chunk_major(inputs["L0_attn_b_attn_norm_w"])
    base["qnw"] = np.asarray(inputs["L0_attn_b_q_norm_w"], np.float32).reshape(1, 128)
    base["knw"] = np.asarray(inputs["L0_attn_b_k_norm_w"], np.float32).reshape(1, 128)
    base["dw1r"] = np.ascontiguousarray(np.asarray(inputs["L0_attn_b_dw1"], np.float32).reshape(2048, 256))
    base["qkwr"] = np.ascontiguousarray(np.asarray(inputs["L0_attn_b_qkw"], np.float32).reshape(4, 64, 64))
    base["ddw"] = np.asarray(inputs["L0_attn_b_dd"], np.float32)
    base["wo"] = np.asarray(inputs["L0_attn_b_wo"], np.float32)
    base["fnw_c"] = _chunk_major(inputs["L0_attn_b_ffn_norm_w"])
    base["w1f"] = np.asarray(inputs["L0_attn_b_w1f"], np.float32)
    base["w3f"] = np.asarray(inputs["L0_attn_b_w3f"], np.float32)
    base["w2f"] = np.asarray(inputs["L0_attn_b_w2f"], np.float32)
    h = core % 2
    pos = h * 2048 + np.arange(2048, dtype=np.float64)
    inv = 1.0 / (10000.0 ** (np.arange(0, 64, dtype=np.float64) / 64.0))
    ang = pos[:, None] * inv[None, :]
    c = np.cos(ang); sn = np.sin(ang)
    sc = 1.0 / np.sqrt(128.0)
    base["cosq"] = (c * sc).astype(np.float32)
    base["sinq"] = (sn * sc).astype(np.float32)
    base["cosk"] = c.astype(np.float32)
    base["sink"] = sn.astype(np.float32)
    am = np.zeros((NTT, 8, 128, 512), dtype=BF16NP)
    for rt in range(NTT):
        grow = h * 2048 + rt * 128 + np.arange(128)[:, None]
        for cbi in range(8):
            gcol = cbi * 512 + np.arange(512)[None, :]
            am[rt, cbi] = np.where(gcol <= grow, np.float32(0.0),
                                   np.float32(-1e30)).astype(BF16NP)
    base["amask"] = am
    return base


def run_l0_attn_b(x_full, inputs, prog=None):
    if prog is None:
        prog = build_l0_attn_b()
    shards = shard_phaseA(x_full)
    in_maps = []
    for c in range(NCORES):
        m = prep_l0_attn_b(inputs, c)
        m["x"] = shards[c]
        in_maps.append(m)
    res = run_bass_kernel_spmd(prog.nc, in_maps, core_ids=list(range(NCORES)))
    return unshard_phaseA([res.results[c]["xout"] for c in range(NCORES)])


def prep_l0_ffn(inputs):
    base = dict(host_consts())
    up_w = np.asarray(inputs["L0_ffn_up_w"], np.float32)
    dn_w = np.asarray(inputs["L0_ffn_down_w"], np.float32)
    base.update({
        "upT": np.ascontiguousarray(up_w.transpose(2, 0, 1).reshape(2048, 8192)),
        "dnT": np.ascontiguousarray(dn_w.transpose(0, 2, 1).reshape(8192, 2048)),
        "upsc_c": np.ascontiguousarray(np.asarray(inputs["L0_ffn_up_scale"], np.float32).T),
        "dsc": np.asarray(inputs["L0_ffn_down_scale"], np.float32),
        "osc": np.asarray(inputs["L0_ffn_output_scale"], np.float32).reshape(64, 1),
        "normw_c": _chunk_major(inputs["L0_ffn_norm_weight"]),
    })
    return base


def run_l0_ffn(x_full, inputs, prog=None):
    base = prep_l0_ffn(inputs)
    if prog is None:
        prog = build_l0_ffn()
    in_maps = []
    for shard in shard_phaseA(x_full):
        m = dict(base); m["x"] = shard
        in_maps.append(m)
    res = run_bass_kernel_spmd(prog.nc, in_maps, core_ids=list(range(NCORES)))
    return unshard_phaseA([res.results[c]["xout"] for c in range(NCORES)])


def shard_phaseA(x_full):
    xt = np.asarray(x_full, np.float32)
    return [np.ascontiguousarray(xt[c // 2, (c % 2) * 2048:((c % 2) + 1) * 2048])
            for c in range(NCORES)]


def unshard_phaseA(shards):
    out = np.empty((B, SEQ, DM), np.float32)
    for c in range(NCORES):
        out[c // 2, (c % 2) * 2048:((c % 2) + 1) * 2048] = shards[c]
    return out




def build_l0_attn_a(upto=9, nrt=NTT, inner=7):
    p = Prog()
    nc = p.nc

    x = nc.dram_tensor("x", [TOK, DM], F32, kind="ExternalInput").ap()
    WqT = nc.dram_tensor("WqT", [DM, DM], F32, kind="ExternalInput").ap()
    WkT = nc.dram_tensor("WkT", [DM, DM], F32, kind="ExternalInput").ap()
    WvT = nc.dram_tensor("WvT", [DM, DM], F32, kind="ExternalInput").ap()
    WgT = nc.dram_tensor("WgT", [DM, DM], F32, kind="ExternalInput").ap()
    WpT = nc.dram_tensor("WpT", [DM, DM], F32, kind="ExternalInput").ap()
    cmask = nc.dram_tensor("cmask", [NTT, 8, 128, 512], BF16, kind="ExternalInput").ap()
    xout = nc.dram_tensor("xout", [TOK, DM], F32, kind="ExternalOutput").ap()
    cst = decl_consts(nc)

    q_scr = nc.dram_tensor("q_scr", [DM, TOK], BF16)
    g_scr = nc.dram_tensor("g_scr", [TOK, DM], BF16)
    kv_loc = nc.dram_tensor("kv_loc", [2 * TOK, DM], BF16)
    kv_all = nc.dram_tensor("kv_all", [4 * TOK, DM], BF16)

    ident = nc.alloc_sbuf_tensor("ident", [128, 128], BF16)
    epsb = nc.alloc_sbuf_tensor("epsb", [128, 4], F32)

    ld = p.sem("ld"); wl = p.sem("wl"); mm = p.sem("mm"); ev = p.sem("ev")
    sty = p.sem("sty"); stg = p.sem("stg"); cc = p.sem("cc"); hl = p.sem("hl")
    y, t, v, s, g = (p.e[k] for k in ('y', 't', 'v', 's', 'g'))

    ld.inc(y.dma_start(out=ident[:], in_=cst["ident"]), 16)
    ld.inc(y.dma_start(out=epsb[:], in_=cst["eps"]), 16)
    for e in (y, t, v, s, g):
        e.wait_ge(ld.h, ld.v)

    sc_a = ScopeAlloc(nc)
    xnT = sc_a.sb("xnT", [128, DCH * TOK], BF16)
    emit_ln_to_xnT(p, x, xnT, ident, epsb[:, 0:1],
                   rms2_eps=1.19e-7, rms2_bias=epsb[:, 2:3])

    sc_b = ScopeAlloc(nc)
    wbufp = [sc_b.sb(f"wbufp{i}", [128, DCH * 512], BF16) for i in range(2)]
    qk_bf = sc_b.sb("qk_bf", [128, TOK], BF16)
    ps_p = [sc_b.ps(f"ps_p{i}", [128, 512]) for i in range(2)]
    p_ring = Ring(p, ps_p, "aap")
    widx = 0
    gcum = []

    def proj_fm(WT, scr, func):
        nonlocal widx
        for oc in range(DCH):
            wb = wbufp[widx % 2]
            if widx >= 2:
                g.wait_ge(p_ring.mm.h, p_ring.mm_base + gcum[widx - 1])
            wl.inc(g.dma_start(out=wb[:, 0:DCH * 128].rearrange("p (c o) -> p c o", c=DCH),
                               in_=WT[:, oc * 128:(oc + 1) * 128]
                               .rearrange("(c p) o -> p c o", p=128)), 16)
            t.wait_ge(wl.h, wl.v)
            for tb in range(4):
                bank = p_ring.produce(t)
                pairs = [(wb[:, k * 128:(k + 1) * 128],
                      xnT[:, k * TOK + tb * 512:k * TOK + (tb + 1) * 512])
                     for k in range(DCH)]
                p_ring.mm_done(mm_accum(t, bank[:], pairs))
                bq = p_ring.consume_wait(s)
                s.wait_ge(sty.h, sty.v)
                act = s.activation(qk_bf[:, tb * 512:(tb + 1) * 512], bq[:], func)
                p_ring.ev_done(act)
            y.wait_ge(p_ring.ev.h, p_ring.ev.v)
            sty.inc(y.dma_start(out=scr[oc * 128:(oc + 1) * 128, :], in_=qk_bf[:]), 16)
            gcum.append(p_ring.i)
            widx += 1

    def proj_tm(WT, scr, func, scr_is_kv2=False):
        nonlocal widx
        for ob in range(4):
            wb = wbufp[widx % 2]
            if widx >= 2:
                g.wait_ge(p_ring.mm.h, p_ring.mm_base + gcum[widx - 1])
            wl.inc(g.dma_start(out=wb[:].rearrange("p (c o) -> p c o", c=DCH),
                               in_=WT[:, ob * 512:(ob + 1) * 512]
                               .rearrange("(c p) o -> p c o", p=128)), 16)
            t.wait_ge(wl.h, wl.v)
            for tt4 in range(4):
                for q4 in range(4):
                    tt = tt4 * 4 + q4
                    bank = p_ring.produce(t)
                    pairs = [(xnT[:, k * TOK + tt * 128:k * TOK + (tt + 1) * 128],
                              wb[:, k * 512:(k + 1) * 512])
                             for k in range(DCH)]
                    p_ring.mm_done(mm_accum(t, bank[:], pairs))
                    bq = p_ring.consume_wait(s)
                    s.wait_ge(sty.h, sty.v)
                    act = s.activation(qk_bf[:, q4 * 512:(q4 + 1) * 512], bq[:], func)
                    p_ring.ev_done(act)
                y.wait_ge(p_ring.ev.h, p_ring.ev.v)
                sty.inc(y.dma_start(
                    out=scr[tt4 * 512:(tt4 + 1) * 512, ob * 512:(ob + 1) * 512]
                        .rearrange("(n q) o -> q n o", n=4),
                    in_=qk_bf[:].rearrange("q (n o) -> q n o", n=4)), 16)
            gcum.append(p_ring.i)
            widx += 1

    AFS = AF.Silu
    proj_fm(WkT, kv_loc[0:TOK, :], AFS)
    proj_tm(WvT, kv_loc[TOK:2 * TOK, :], AFS)
    proj_fm(WqT, q_scr, AFS)
    proj_tm(WgT, g_scr, AF.Sigmoid)
    sc_b.close()
    sc_a.close()
    p.barrier()

    if upto < 2:
        g.wait_ge(sty.h, sty.v)
        p.compile_block()
        return p
    g.wait_ge(sty.h, sty.v)
    cci = g.collective_compute(
        "AllGather", ALU.bypass,
        replica_groups=[[0, 1], [2, 3], [4, 5], [6, 7]],
        ins=[kv_loc[:]], outs=[kv_all[:]])
    cc.inc(cci, 1)
    for e in (y, t, v, s, g):
        e.wait_ge(cc.h, cc.v)

    def kfm_tile(kchunk, c0, w):
        r = c0 // TOK
        cl = c0 % TOK
        return kv_all[r * 2 * TOK + kchunk * 128: r * 2 * TOK + (kchunk + 1) * 128,
                      cl:cl + w] if False else                kv_all[r * 2 * TOK + kchunk * 128: r * 2 * TOK + kchunk * 128 + 128, cl:cl + w]

    def vtm_tile(s0, w):
        r = s0 // TOK
        sl = s0 % TOK
        return kv_all[r * 2 * TOK + TOK + sl: r * 2 * TOK + TOK + sl + w, :]

    if upto < 3:
        p.compile_block()
        return p
    sc_c = ScopeAlloc(nc)
    WpTs = sc_c.sb("WpTs", [128, DCH * DM], BF16)
    qrt = sc_c.sb("qrt", [128, DCH * 128], BF16)
    kbuf = [sc_c.sb(f"kbuf{i}", [128, DCH * 512], BF16) for i in range(2)]
    vbuf = [sc_c.sb(f"vbuf{i}", [128, 4 * DM], BF16) for i in range(2)]
    sbf = sc_c.sb("sbf", [128, 512], BF16)
    sbt = sc_c.sb("sbt", [128, 512], BF16)
    gt = sc_c.sb("gt", [128, DM], BF16)
    ot = sc_c.sb("ot", [128, DM], BF16)
    oT = sc_c.sb("oT", [128, DCH * 128], BF16)
    xt2 = sc_c.sb("xt2", [128, DM], F32)
    outf = sc_c.sb("outf", [128, 512], F32)
    ps_o = [sc_c.ps(f"ps_ao{i}", [128, 512]) for i in range(4)]
    ps_s = sc_c.ps("ps_as", [128, 512])
    ps_t = [sc_c.ps(f"ps_at{i}", [128, 128], BF16) for i in range(2)]

    wlg = p.sem("wlg")
    for c2 in range(DCH):
        wlg.inc(g.dma_start(out=WpTs[:, c2 * DM:(c2 + 1) * DM],
                            in_=WpT[c2 * 128:(c2 + 1) * 128, :]), 16)
    t.wait_ge(wlg.h, wlg.v)

    mrt = sc_c.sb("mrt", [128, 8 * 512], BF16)
    for rt in range(nrt):
        y.wait_ge(ev.h, ev.v)
        hl.inc(y.dma_start(out=mrt[:].rearrange("p (n o) -> p n o", n=8),
                           in_=cmask[rt].rearrange("n p o -> p n o")), 16)
        hl.inc(y.dma_start(out=qrt[:].rearrange("p (c q) -> p c q", c=DCH),
                           in_=q_scr[:, rt * 128:(rt + 1) * 128]
                           .rearrange("(c p) q -> p c q", p=128)), 16)
        hl.inc(y.dma_start(out=gt[:], in_=g_scr[rt * 128:(rt + 1) * 128, :]), 16)
        hl.inc(y.dma_start(out=xt2[:], in_=x[rt * 128:(rt + 1) * 128, :]), 16)
        ncb = (16 + rt + 1 + 3) // 4
        t.wait_ge(hl.h, hl.v)
        tr_ring = Ring(p, ps_t, "atr")
        for cb in range(ncb):
            kb = kbuf[cb % 2]
            vb = vbuf[cb % 2]
            r = (cb * 512) // TOK
            cl = (cb * 512) % TOK
            y.wait_ge(mm.h, mm.v)
            for kchunk in range(DCH):
                hl.inc(y.dma_start(out=kb[:, kchunk * 512:(kchunk + 1) * 512],
                                   in_=kv_all[r * 2 * TOK + kchunk * 128:
                                              r * 2 * TOK + kchunk * 128 + 128,
                                              cl:cl + 512]), 16)
            hl.inc(y.dma_start(out=vb[:].rearrange("p (n d) -> p n d", n=4),
                               in_=kv_all[r * 2 * TOK + TOK + cl:
                                          r * 2 * TOK + TOK + cl + 512, :]
                               .rearrange("(n p) d -> p n d", p=128)), 16)
            t.wait_ge(hl.h, hl.v)
            pairs = [(qrt[:, k * 128:(k + 1) * 128], kb[:, k * 512:(k + 1) * 512])
                     for k in range(DCH)]
            mm.inc(mm_accum(t, ps_s[:], pairs), 1)
            v.wait_ge(mm.h, mm.v)
            ev.inc(v.tensor_mul(sbf[:], ps_s[:], mrt[:, cb * 512:(cb + 1) * 512]), 1)
            t.wait_ge(ev.h, ev.v)
            if inner < 2:
                continue
            for st4 in range(4):
                bank = tr_ring.produce(t)
                tr_ring.mm_done(t.transpose(bank[:], sbf[:, st4 * 128:(st4 + 1) * 128],
                                            ident[:]))
                bb = tr_ring.consume_wait(v)
                tr_ring.ev_done(v.tensor_copy(sbt[:, st4 * 128:(st4 + 1) * 128], bb[:]))
            t.wait_ge(tr_ring.ev.h, tr_ring.ev.v)
            if inner < 3:
                continue
            for eb in range(4):
                pairs = [(sbt[:, st4 * 128:(st4 + 1) * 128],
                          vb[:, st4 * (DM) + eb * 512: st4 * DM + (eb + 1) * 512])
                         for st4 in range(4)]
                lastm = mm_accum_cont(t, ps_o[eb][:], pairs, first=(cb == 0),
                                      last=(cb == ncb - 1))
            mm.inc(lastm, 1)
        if inner < 4:
            continue
        v.wait_ge(mm.h, mm.v)
        for eb in range(4):
            ev.inc(v.tensor_mul(ot[:, eb * 512:(eb + 1) * 512], ps_o[eb][:],
                                gt[:, eb * 512:(eb + 1) * 512]), 1)
        if inner >= 5:
            t.wait_ge(ev.h, ev.v)
            tr2 = Ring(p, ps_t, "atr")
            for c2 in range(DCH):
                bank = tr2.produce(t)
                tr2.mm_done(t.transpose(bank[:], ot[:, c2 * 128:(c2 + 1) * 128], ident[:]))
                bb = tr2.consume_wait(v)
                tr2.ev_done(v.tensor_copy(oT[:, c2 * 128:(c2 + 1) * 128], bb[:]))
            t.wait_ge(tr2.ev.h, tr2.ev.v)
        if inner >= 6:
            for ob in range(4):
                pairs = [(oT[:, c2 * 128:(c2 + 1) * 128],
                          WpTs[:, c2 * DM + ob * 512:c2 * DM + (ob + 1) * 512])
                         for c2 in range(DCH)]
                mm.inc(mm_accum(t, ps_s[:], pairs), 1)
                v.wait_ge(mm.h, mm.v)
                v.wait_ge(sty.h, sty.v)
                ev.inc(v.tensor_add(outf[:], ps_s[:], xt2[:, ob * 512:(ob + 1) * 512]), 1)
                y.wait_ge(ev.h, ev.v)
                sty.inc(y.dma_start(out=xout[rt * 128:(rt + 1) * 128, ob * 512:(ob + 1) * 512],
                                    in_=outf[:]), 16)
                t.wait_ge(ev.h, ev.v)
    y.wait_ge(sty.h, sty.v)
    sc_c.close()
    p.barrier()
    p.compile_block()
    return p


def mm_accum_cont(t, psum_ap, pairs, first, last):
    n = len(pairs)
    lastm = None
    for i, (l, r) in enumerate(pairs):
        lastm = t.matmul(psum_ap, l, r, start=(first and i == 0),
                         stop=(last and i == n - 1), skip_group_check=True)
    return lastm




def emit_rope_tm(p, src_f32, dst_bf, cos_ap, sin_ap, nheads, scr):
    v = p.e['v']
    lo = src_f32.rearrange("p (h z) -> p h z", h=nheads)[:, :, 0:64]
    hi = src_f32.rearrange("p (h z) -> p h z", h=nheads)[:, :, 64:128]
    dlo = dst_bf.rearrange("p (h z) -> p h z", h=nheads)[:, :, 0:64]
    dhi = dst_bf.rearrange("p (h z) -> p h z", h=nheads)[:, :, 64:128]
    s1 = scr.rearrange("p (h z) -> p h z", h=nheads)[:, :, 0:64]
    s2 = scr.rearrange("p (h z) -> p h z", h=nheads)[:, :, 64:128]
    cb = cos_ap.unsqueeze(1).to_broadcast((128, nheads, 64))
    sb = sin_ap.unsqueeze(1).to_broadcast((128, nheads, 64))
    p.vsync(v.tensor_tensor(s1, lo, cb, ALU.mult))
    p.vsync(v.tensor_tensor(s2, hi, sb, ALU.mult))
    p.vsync(v.tensor_tensor(dlo, s1, s2, ALU.subtract))
    p.vsync(v.tensor_tensor(s1, hi, cb, ALU.mult))
    p.vsync(v.tensor_tensor(s2, lo, sb, ALU.mult))
    return p.vsync(v.tensor_tensor(dhi, s1, s2, ALU.add))


def build_l1_attn_b():
    p = Prog()
    nc = p.nc

    x = nc.dram_tensor("x", [TOK, DM], F32, kind="ExternalInput").ap()
    Wq = nc.dram_tensor("Wq", [DM, DM], F32, kind="ExternalInput").ap()
    Wk = nc.dram_tensor("Wk", [DM, 512], F32, kind="ExternalInput").ap()
    Wv = nc.dram_tensor("Wv", [DM, 512], F32, kind="ExternalInput").ap()
    Wo = nc.dram_tensor("Wo", [DM, DM], F32, kind="ExternalInput").ap()
    cq = nc.dram_tensor("cq", [512, 64], F32, kind="ExternalInput").ap()
    sq = nc.dram_tensor("sq", [512, 64], F32, kind="ExternalInput").ap()
    ck = nc.dram_tensor("ck", [512, 64], F32, kind="ExternalInput").ap()
    sk = nc.dram_tensor("sk", [512, 64], F32, kind="ExternalInput").ap()
    amask = nc.dram_tensor("amask", [4, 128, 384], BF16, kind="ExternalInput").ap()
    nbr = nc.dram_tensor("nbr", [1, 8], I32, kind="ExternalInput").ap()
    xout = nc.dram_tensor("xout", [TOK, DM], F32, kind="ExternalOutput").ap()
    cst = decl_consts(nc)

    q_scr = nc.dram_tensor("q_scr", [DM, TOK], BF16)
    kv_loc = nc.dram_tensor("kv_loc", [B, 2, 512, 512], BF16)
    bd_loc = nc.dram_tensor("bd_loc", [B * 2 * 256, 512], BF16)
    bd_all = nc.dram_tensor("bd_all", [NCORES * B * 2 * 256, 512], BF16)

    ident = nc.alloc_sbuf_tensor("ident", [128, 128], BF16)
    epsb = nc.alloc_sbuf_tensor("epsb", [128, 4], F32)
    amk = nc.alloc_sbuf_tensor("amk", [128, 4 * 384], BF16)
    ctabs = nc.alloc_sbuf_tensor("ctabs", [128, 4 * 256], F32)
    nbrs = nc.alloc_sbuf_tensor("nbrs", [1, 8], I32)

    ld = p.sem("ld"); wl = p.sem("wl"); mm = p.sem("mm"); ev = p.sem("ev")
    sty = p.sem("sty"); cc = p.sem("cc"); hl = p.sem("hl")
    y, t, v, s, g = (p.e[k] for k in ('y', 't', 'v', 's', 'g'))

    ld.inc(y.dma_start(out=ident[:], in_=cst["ident"]), 16)
    ld.inc(y.dma_start(out=epsb[:], in_=cst["eps"]), 16)
    ld.inc(y.dma_start(out=amk[:].rearrange("p (n o) -> p n o", n=4),
                       in_=amask.rearrange("n p o -> p n o")), 16)
    ld.inc(y.dma_start(out=nbrs[:], in_=nbr), 16)
    for pr in range(4):
        ld.inc(y.dma_start(out=ctabs[:, pr * 256 + 0:pr * 256 + 64],
                           in_=cq[pr * 128:(pr + 1) * 128, :]), 16)
        ld.inc(y.dma_start(out=ctabs[:, pr * 256 + 64:pr * 256 + 128],
                           in_=sq[pr * 128:(pr + 1) * 128, :]), 16)
        ld.inc(y.dma_start(out=ctabs[:, pr * 256 + 128:pr * 256 + 192],
                           in_=ck[pr * 128:(pr + 1) * 128, :]), 16)
        ld.inc(y.dma_start(out=ctabs[:, pr * 256 + 192:pr * 256 + 256],
                           in_=sk[pr * 128:(pr + 1) * 128, :]), 16)
    for e in (y, t, v, s, g):
        e.wait_ge(ld.h, ld.v)

    sc_a = ScopeAlloc(nc)
    xnT = sc_a.sb("xnT", [128, DCH * TOK], BF16)
    emit_ln_to_xnT(p, x, xnT, ident, epsb[:, 0:1])

    sc_b = ScopeAlloc(nc)
    wqs = sc_b.sb("wqs", [128, DCH * DM], BF16)
    wkvs = sc_b.sb("wkvs", [128, DCH * 1024], BF16)
    qt = sc_b.sb("qt", [128, DM], F32)
    qrop = sc_b.sb("qrop", [128, DM], BF16)
    ropescr = sc_b.sb("ropescr", [128, DM], F32)
    kvt = sc_b.sb("kvt", [128, 1024], F32)
    kvrop = sc_b.sb("kvrop", [128, 1024], BF16)
    qfm_bf = sc_b.sb("qfm_bf", [128, DCH * 128], BF16)
    ps_a = [sc_b.ps(f"ps_a{i}", [128, 512]) for i in range(2)]
    ps_t = [sc_b.ps(f"ps_t{i}", [128, 128], BF16) for i in range(2)]
    a_ring = Ring(p, ps_a, "l1b")
    for k in range(DCH):
        wl.inc(g.dma_start(out=wqs[:, k * DM:(k + 1) * DM], in_=Wq[k * 128:(k + 1) * 128, :]), 16)
        wl.inc(g.dma_start(out=wkvs[:, k * 1024:k * 1024 + 512], in_=Wk[k * 128:(k + 1) * 128, :]), 16)
        wl.inc(g.dma_start(out=wkvs[:, k * 1024 + 512:(k + 1) * 1024], in_=Wv[k * 128:(k + 1) * 128, :]), 16)
    t.wait_ge(wl.h, wl.v)
    for tt in range(NTT):
        bseq = tt // 4
        pr = tt % 4
        xn_t = [xnT[:, k * TOK + tt * 128:k * TOK + (tt + 1) * 128] for k in range(DCH)]
        for ob in range(4):
            bank = a_ring.produce(t)
            a_ring.mm_done(mm_accum(t, bank[:], [(xn_t[k], wqs[:, k * DM + ob * 512:k * DM + (ob + 1) * 512]) for k in range(DCH)]))
            bq = a_ring.consume_wait(v)
            a_ring.ev_done(v.tensor_copy(qt[:, ob * 512:(ob + 1) * 512], bq[:]))
        v.wait_ge(a_ring.ev.h, a_ring.ev.v)
        v.wait_ge(sty.h, sty.v)
        emit_rope_tm(p, qt[:], qrop[:], ctabs[:, pr * 256:pr * 256 + 64],
                     ctabs[:, pr * 256 + 64:pr * 256 + 128], 16, ropescr[:])
        tr = Ring(p, ps_t, "l1btr")
        sv = p.sem("selfsync_v")
        t.wait_ge(sv.h, sv.v)
        for c2 in range(DCH):
            bank = tr.produce(t)
            tr.mm_done(t.transpose(bank[:], qrop[:, c2 * 128:(c2 + 1) * 128], ident[:]))
            bb = tr.consume_wait(v)
            tr.ev_done(v.tensor_copy(qfm_bf[:, c2 * 128:(c2 + 1) * 128], bb[:]))
        y.wait_ge(tr.ev.h, tr.ev.v)
        sty.inc(y.dma_start(out=q_scr[:, tt * 128:(tt + 1) * 128]
                            .rearrange("(c p) q -> p c q", p=128),
                            in_=qfm_bf[:].rearrange("p (c q) -> p c q", c=DCH)), 16)
        bank = a_ring.produce(t)
        a_ring.mm_done(mm_accum(t, bank[:], [(xn_t[k], wkvs[:, k * 1024:k * 1024 + 512]) for k in range(DCH)]))
        bk = a_ring.consume_wait(v)
        a_ring.ev_done(v.tensor_copy(kvt[:, 0:512], bk[:]))
        bank = a_ring.produce(t)
        a_ring.mm_done(mm_accum(t, bank[:], [(xn_t[k], wkvs[:, k * 1024 + 512:(k + 1) * 1024]) for k in range(DCH)]))
        bv = a_ring.consume_wait(v)
        a_ring.ev_done(v.tensor_copy(kvt[:, 512:1024], bv[:]))
        v.wait_ge(a_ring.ev.h, a_ring.ev.v)
        emit_rope_tm(p, kvt[:, 0:512], kvrop[:, 0:512],
                     ctabs[:, pr * 256 + 128:pr * 256 + 192],
                     ctabs[:, pr * 256 + 192:pr * 256 + 256], 4, ropescr[:, 0:512])
        p.vsync(v.tensor_copy(kvrop[:, 512:1024], kvt[:, 512:1024]))
        y.wait_ge(sv.h, sv.v)
        sty.inc(y.dma_start(out=kv_loc[bseq, 0, pr * 128:(pr + 1) * 128, :], in_=kvrop[:, 0:512]), 16)
        sty.inc(y.dma_start(out=kv_loc[bseq, 1, pr * 128:(pr + 1) * 128, :], in_=kvrop[:, 512:1024]), 16)
        if pr >= 2:
            boff = bseq * 256 + (pr - 2) * 128
            sty.inc(y.dma_start(out=bd_loc[boff:boff + 128, :], in_=kvrop[:, 0:512]), 16)
            sty.inc(y.dma_start(out=bd_loc[B * 256 + boff:B * 256 + boff + 128, :], in_=kvrop[:, 512:1024]), 16)
    sc_b.close()
    sc_a.close()
    p.barrier()

    g.wait_ge(sty.h, sty.v)
    cc.inc(g.collective_compute("AllGather", ALU.bypass,
                                replica_groups=[[0, 1, 2, 3, 4, 5, 6, 7]],
                                ins=[bd_loc[:]], outs=[bd_all[:]]), 1)
    for e in (y, t, v, s, g):
        e.wait_ge(cc.h, cc.v)

    sc_c = ScopeAlloc(nc)
    wos = sc_c.sb("wos", [128, DCH * DM], BF16)
    qrt = sc_c.sb("qrt", [128, DCH * 128], BF16)
    kwin = sc_c.sb("kwin", [128, 3 * 512], BF16)
    vwin = sc_c.sb("vwin", [128, 3 * 512], BF16)
    kwinT = sc_c.sb("kwinT", [128, 4 * 384], BF16)
    lrow = sc_c.sb("lrow", [128, 384], F32)
    prow_b = sc_c.sb("prow_b", [128, 384], BF16)
    pT = sc_c.sb("pT", [128, 384], BF16)
    ot = sc_c.sb("ot", [128, DM], BF16)
    oT = sc_c.sb("oT", [128, DCH * 128], BF16)
    xt2 = sc_c.sb("xt2", [128, DM], F32)
    outf = sc_c.sb("outf", [128, 512], F32)
    stats = sc_c.sb("l1bstats", [128, 4], F32)
    ps_o = sc_c.ps("ps_bo", [128, 128])
    ps_s = sc_c.ps("ps_bs", [128, 512])
    ps_t2 = [sc_c.ps(f"ps_bt{i}", [128, 128], BF16) for i in range(2)]
    wlg = p.sem("wlg")
    for k in range(DCH):
        wlg.inc(g.dma_start(out=wos[:, k * DM:(k + 1) * DM], in_=Wo[k * 128:(k + 1) * 128, :]), 16)
    t.wait_ge(wlg.h, wlg.v)


    for bseq in range(B):
        for rb in range(4):
            t0 = rb * 128
            y.wait_ge(mm.h, mm.v)
            y.wait_ge(ev.h, ev.v)
            hl.inc(y.dma_start(out=qrt[:].rearrange("p (c q) -> p c q", c=DCH),
                               in_=q_scr[:, (bseq * 4 + rb) * 128:(bseq * 4 + rb) * 128 + 128]
                               .rearrange("(c p) q -> p c q", p=128)), 16)
            hl.inc(y.dma_start(out=xt2[:], in_=x[(bseq * 4 + rb) * 128:(bseq * 4 + rb) * 128 + 128, :]), 16)
            for cbk in range(3):
                lpos = t0 - 256 + cbk * 128
                if lpos >= 0:
                    hl.inc(y.dma_start(out=kwin[:, cbk * 512:(cbk + 1) * 512],
                                       in_=kv_loc[bseq, 0, lpos:lpos + 128, :]), 16)
                    hl.inc(y.dma_start(out=vwin[:, cbk * 512:(cbk + 1) * 512],
                                       in_=kv_loc[bseq, 1, lpos:lpos + 128, :]), 16)
                else:
                    def halo_load(eng, cbk=cbk, bseq=bseq, lpos=lpos):
                        r0 = bseq * 256 + (256 + lpos)
                        with eng.register(f"rnb_{bseq}_{lpos}_{cbk}") as rnb:
                            eng.reg_load(rnb, nbrs[0:1, 0:1])
                            eng.reg_mul(rnb, rnb, B * 2 * 256 * 512)
                            eng.reg_add(rnb, rnb, r0 * 512)
                            eng.dma_start(
                                out=kwin[:, cbk * 512:(cbk + 1) * 512],
                                in_=bass.AP(bd_all, rnb, [[512, 128], [1, 512]])
                            ).then_inc(hl.h, 16)
                            eng.reg_add(rnb, rnb, B * 256 * 512)
                            eng.dma_start(
                                out=vwin[:, cbk * 512:(cbk + 1) * 512],
                                in_=bass.AP(bd_all, rnb, [[512, 128], [1, 512]])
                            ).then_inc(hl.h, 16)
                    y.raw(halo_load)
                    hl.v += 32
            t.wait_ge(hl.h, hl.v)
            tr = Ring(p, ps_t2, "l1btr")
            for kvh in range(4):
                for cbk in range(3):
                    bank = tr.produce(t)
                    tr.mm_done(t.transpose(bank[:], kwin[:, cbk * 512 + kvh * 128:cbk * 512 + (kvh + 1) * 128], ident[:]))
                    bb = tr.consume_wait(v)
                    tr.ev_done(v.tensor_copy(kwinT[:, kvh * 384 + cbk * 128:kvh * 384 + (cbk + 1) * 128], bb[:]))
            t.wait_ge(tr.ev.h, tr.ev.v)
            for nh in range(16):
                kvh = nh // 4
                mm.inc(t.matmul(ps_s[:, 0:384], qrt[:, nh * 128:(nh + 1) * 128],
                                kwinT[:, kvh * 384:(kvh + 1) * 384], start=True, stop=True), 1)
                v.wait_ge(mm.h, mm.v)
                p.vsync(v.tensor_tensor(lrow[:], ps_s[:, 0:384], amk[:, rb * 384:(rb + 1) * 384], ALU.add))
                p.vsync(v.reduce_max(stats[:, 0:1], lrow[:], axis=AX.X))
                p.vsync(v.tensor_scalar_mul(stats[:, 1:2], stats[:, 0:1], -1.0))
                lnx = p.sem("lnx")
                sv2 = p.sem("selfsync_v")
                s.wait_ge(sv2.h, sv2.v)
                lnx.inc(s.activation(prow_b[:], lrow[:], AF.Exp, bias=stats[:, 1:2]), 1)
                v.wait_ge(lnx.h, lnx.v)
                p.vsync(v.reduce_sum(stats[:, 2:3], prow_b[:], axis=AX.X))
                p.vsync(v.reciprocal(stats[:, 3:4], stats[:, 2:3]))
                p.vsync(v.tensor_scalar_mul(prow_b[:], prow_b[:], stats[:, 3:4]))
                t.wait_ge(p.sem("selfsync_v").h, p.sem("selfsync_v").v)
                tr2 = Ring(p, ps_t2, "l1btr")
                for cbk in range(3):
                    bank = tr2.produce(t)
                    tr2.mm_done(t.transpose(bank[:], prow_b[:, cbk * 128:(cbk + 1) * 128], ident[:]))
                    bb = tr2.consume_wait(v)
                    tr2.ev_done(v.tensor_copy(pT[:, cbk * 128:(cbk + 1) * 128], bb[:]))
                t.wait_ge(tr2.ev.h, tr2.ev.v)
                lastm = mm_accum(t, ps_o[:], [(pT[:, cbk * 128:(cbk + 1) * 128],
                                               vwin[:, cbk * 512 + kvh * 128:cbk * 512 + (kvh + 1) * 128])
                                              for cbk in range(3)])
                mm.inc(lastm, 1)
                v.wait_ge(mm.h, mm.v)
                ev.inc(v.tensor_copy(ot[:, nh * 128:(nh + 1) * 128], ps_o[:]), 1)
                t.wait_ge(ev.h, ev.v)
            tr3 = Ring(p, ps_t2, "l1btr")
            for c2 in range(DCH):
                bank = tr3.produce(t)
                tr3.mm_done(t.transpose(bank[:], ot[:, c2 * 128:(c2 + 1) * 128], ident[:]))
                bb = tr3.consume_wait(v)
                tr3.ev_done(v.tensor_copy(oT[:, c2 * 128:(c2 + 1) * 128], bb[:]))
            t.wait_ge(tr3.ev.h, tr3.ev.v)
            for ob in range(4):
                lastm = mm_accum(t, ps_s[:], [(oT[:, c2 * 128:(c2 + 1) * 128],
                                               wos[:, c2 * DM + ob * 512:c2 * DM + (ob + 1) * 512])
                                              for c2 in range(DCH)])
                mm.inc(lastm, 1)
                v.wait_ge(mm.h, mm.v)
                v.wait_ge(sty.h, sty.v)
                ev.inc(v.tensor_add(outf[:], ps_s[:], xt2[:, ob * 512:(ob + 1) * 512]), 1)
                y.wait_ge(ev.h, ev.v)
                sty.inc(y.dma_start(out=xout[(bseq * 4 + rb) * 128:(bseq * 4 + rb) * 128 + 128,
                                             ob * 512:(ob + 1) * 512], in_=outf[:]), 16)
                t.wait_ge(ev.h, ev.v)
    y.wait_ge(sty.h, sty.v)
    sc_c.close()
    p.barrier()
    p.compile_block()
    return p


def prep_l1_attn_b(inputs, core):
    base = dict(host_consts())
    base["Wq"] = np.asarray(inputs["L1_attn_b_Wq"], np.float32)
    base["Wk"] = np.asarray(inputs["L1_attn_b_Wk"], np.float32)
    base["Wv"] = np.asarray(inputs["L1_attn_b_Wv"], np.float32)
    base["Wo"] = np.asarray(inputs["L1_attn_b_Wo"], np.float32)
    HEAD_DIM = 128
    inv = 1.0 / 10000.0 ** (np.arange(0, HEAD_DIM, 2, dtype=np.float64) / HEAD_DIM)
    pos = core * 512 + np.arange(512, dtype=np.float64)
    fr = np.outer(pos, inv)
    c = np.cos(fr).astype(np.float32); sn = np.sin(fr).astype(np.float32)
    sc = 1.0 / np.sqrt(HEAD_DIM)
    base["cq"] = (c * sc).astype(np.float32)
    base["sq"] = (sn * sc).astype(np.float32)
    base["ck"] = c; base["sk"] = sn
    am = np.zeros((4, 128, 384), dtype=BF16NP)
    for rb in range(4):
        grow = core * 512 + rb * 128 + np.arange(128)[:, None]
        gcol = core * 512 + rb * 128 - 256 + np.arange(384)[None, :]
        ok = (gcol >= 0) & (grow - gcol >= 0) & (grow - gcol < 256)
        am[rb] = np.where(ok, np.float32(0.0), np.float32(-1e30)).astype(BF16NP)
    base["amask"] = am
    nb = np.zeros((1, 8), np.int32); nb[0, 0] = (core - 1) % NCORES
    base["nbr"] = nb
    return base


def run_l1_attn_b(x_full, inputs, prog=None):
    if prog is None:
        prog = build_l1_attn_b()
    shards = shard_phaseB(x_full)
    in_maps = []
    for c in range(NCORES):
        m = prep_l1_attn_b(inputs, c)
        m["x"] = shards[c]
        in_maps.append(m)
    res = run_bass_kernel_spmd(prog.nc, in_maps, core_ids=list(range(NCORES)))
    return unshard_phaseB([res.results[c]["xout"] for c in range(NCORES)])



def build_l1_attn_a():
    p = Prog()
    nc = p.nc
    QO = 6144

    x = nc.dram_tensor("x", [TOK, DM], F32, kind="ExternalInput").ap()
    wqkvT = nc.dram_tensor("wqkvT", [DM, QO], F32, kind="ExternalInput").ap()
    qbias = nc.dram_tensor("qbias", [1, QO], F32, kind="ExternalInput").ap()
    lnw_c = nc.dram_tensor("lnw_c", [128, DCH], F32, kind="ExternalInput").ap()
    lnb_c = nc.dram_tensor("lnb_c", [128, DCH], F32, kind="ExternalInput").ap()
    projT = nc.dram_tensor("projT", [DM, DM], F32, kind="ExternalInput").ap()
    pbias = nc.dram_tensor("pbias", [1, DM], F32, kind="ExternalInput").ap()
    xout = nc.dram_tensor("xout", [TOK, DM], F32, kind="ExternalOutput").ap()
    cst = decl_consts(nc)
    qkv_scr = nc.dram_tensor("qkv_scr", [TOK, QO], BF16)

    ident = nc.alloc_sbuf_tensor("ident", [128, 128], BF16)
    epsb = nc.alloc_sbuf_tensor("epsb", [128, 4], F32)
    ones_row = nc.alloc_sbuf_tensor("ones_row", [1, 128], BF16)
    lnwc = nc.alloc_sbuf_tensor("lnwc", [128, DCH], F32)
    lnbc = nc.alloc_sbuf_tensor("lnbc", [128, DCH], F32)
    qb_f = nc.alloc_sbuf_tensor("qb_f", [1, QO], F32)
    brow = nc.alloc_sbuf_tensor("brow", [1, QO], BF16)
    pb_f = nc.alloc_sbuf_tensor("pb_f", [1, DM], F32)
    pbrow = nc.alloc_sbuf_tensor("pbrow", [1, DM], BF16)

    ld = p.sem("ld"); wl = p.sem("wl"); mm = p.sem("mm"); ev = p.sem("ev")
    sty = p.sem("sty"); hl = p.sem("hl")
    y, t, v, s, g = (p.e[k] for k in ('y', 't', 'v', 's', 'g'))

    ld.inc(y.dma_start(out=ident[:], in_=cst["ident"]), 16)
    ld.inc(y.dma_start(out=epsb[:], in_=cst["eps"]), 16)
    ld.inc(y.dma_start(out=ones_row[:], in_=cst["ones_row"]), 16)
    ld.inc(y.dma_start(out=lnwc[:], in_=lnw_c), 16)
    ld.inc(y.dma_start(out=lnbc[:], in_=lnb_c), 16)
    ld.inc(y.dma_start(out=qb_f[:], in_=qbias), 16)
    ld.inc(y.dma_start(out=pb_f[:], in_=pbias), 16)
    for e in (y, t, v, s, g):
        e.wait_ge(ld.h, ld.v)
    ev.inc(v.tensor_copy(pbrow[:], pb_f[:]), 1)
    lnbc_bf = nc.alloc_sbuf_tensor("lnbc_bf", [128, DCH], BF16)
    ev.inc(v.tensor_copy(lnbc_bf[:], lnbc[:]), 1)

    sc_a = ScopeAlloc(nc)
    xnT = sc_a.sb("xnT", [128, DCH * TOK], BF16)
    emit_ln_to_xnT(p, x, xnT, ident, epsb[:, 0:1],
                   rms2_eps=1e-5, rms2_bias=epsb[:, 0:1])

    sc_b = ScopeAlloc(nc)
    wtile = sc_b.sb("wtile", [128, DCH * 512], BF16)
    qkv_bf = sc_b.sb("qkv_bf", [128, 512], BF16)
    ps_b = sc_b.ps("ps_b", [1, 512])
    ps_q = [sc_b.ps(f"ps_q{i}", [128, 512]) for i in range(2)]
    q_ring = Ring(p, ps_q, "l1aq")
    lnx = p.sem("lnx")
    for ob in range(12):
        g.wait_ge(q_ring.mm.h, q_ring.mm.v)
        for k in range(DCH):
            wl.inc(g.dma_start(out=wtile[:, k * 512:(k + 1) * 512],
                               in_=wqkvT[k * 128:(k + 1) * 128, ob * 512:(ob + 1) * 512]), 16)
        t.wait_ge(wl.h, wl.v)
        pairs = [(lnbc_bf[:, k:k + 1], wtile[:, k * 512:(k + 1) * 512]) for k in range(DCH)]
        mm.inc(mm_accum(t, ps_b[:], pairs), 1)
        v.wait_ge(mm.h, mm.v)
        p.vsync(v.tensor_add(brow[:, ob * 512:(ob + 1) * 512], ps_b[:],
                             qb_f[:, ob * 512:(ob + 1) * 512]))
        for k in range(DCH):
            ev.inc(v.tensor_scalar_mul(wtile[:, k * 512:(k + 1) * 512],
                                       wtile[:, k * 512:(k + 1) * 512], lnwc[:, k:k + 1]), 1)
        t.wait_ge(ev.h, ev.v)
        sv2 = p.sem("selfsync_v")
        t.wait_ge(sv2.h, sv2.v)
        for tt in range(NTT):
            bank = q_ring.produce(t)
            pairs = [(xnT[:, k * TOK + tt * 128:k * TOK + (tt + 1) * 128],
                      wtile[:, k * 512:(k + 1) * 512]) for k in range(DCH)]
            pairs.append((ones_row[:], brow[:, ob * 512:(ob + 1) * 512]))
            q_ring.mm_done(mm_accum(t, bank[:], pairs))
            bq = q_ring.consume_wait(v)
            v.wait_ge(sty.h, sty.v)
            q_ring.ev_done(v.tensor_copy(qkv_bf[:], bq[:]))
            y.wait_ge(q_ring.ev.h, q_ring.ev.v)
            sty.inc(y.dma_start(out=qkv_scr[tt * 128:(tt + 1) * 128, ob * 512:(ob + 1) * 512],
                                in_=qkv_bf[:]), 16)
    sc_b.close()
    sc_a.close()
    p.barrier()

    sc_c = ScopeAlloc(nc)
    prjs = sc_c.sb("prjs", [128, DCH * DM], BF16)
    qkvg = [sc_c.sb(f"qkvg{i}", [128, QO], BF16) for i in range(4)]
    scrap2 = sc_c.sb("scrap2", [128, 128], F32)
    sc_s = [sc_c.sb(f"sc_s{i}", [128, 64], F32) for i in range(4)]
    mbuf = sc_c.sb("mbuf", [128, 16], F32)
    ebuf = sc_c.sb("ebuf", [128, 64], F32)
    zbuf = sc_c.sb("zbuf", [128, 16], F32)
    pbuf = sc_c.sb("pbuf", [128, 64], F32)
    ctx = [sc_c.sb(f"ctx{i}", [128, DM], F32) for i in range(2)]
    ctmp = sc_c.sb("ctmp", [128, DM], F32)
    ctx_bf = sc_c.sb("ctx_bf", [128, DM], BF16)
    ctxT = sc_c.sb("ctxT", [128, DCH * 128], BF16)
    xt2 = sc_c.sb("xt2", [128, DM], F32)
    outf = sc_c.sb("outf", [128, 512], F32)
    ps_o = sc_c.ps("ps_ao", [128, 512])
    ps_t = [sc_c.ps(f"ps_at{i}", [128, 128], BF16) for i in range(2)]
    wlg = p.sem("wlg")
    for k in range(DCH):
        wlg.inc(g.dma_start(out=prjs[:, k * DM:(k + 1) * DM],
                            in_=projT[k * 128:(k + 1) * 128, :]), 16)
    t.wait_ge(wlg.h, wlg.v)
    SCL = float(1.0 / np.sqrt(128.0))
    ev.inc(v.memset(ebuf[:], 0.0), 1)
    for grp in range(4):
        for sb in range(4):
            hl.inc(y.dma_start(out=qkvg[sb][:],
                               in_=qkv_scr[(sb * 4 + grp) * 128:(sb * 4 + grp) * 128 + 128, :]), 16)
        v.wait_ge(hl.h, hl.v)
        for sq in range(4):
            for sk in range(sq + 1):
                for h in range(16):
                    p.vsync(v.tensor_mul(scrap2[:],
                                         qkvg[sq][:, h * 128:(h + 1) * 128],
                                         qkvg[sk][:, 2048 + h * 128:2048 + (h + 1) * 128]))
                    p.vsync(v.reduce_sum(sc_s[sq][:, h * 4 + sk:h * 4 + sk + 1],
                                         scrap2[:], axis=AX.X))
            p.vsync(v.tensor_scalar_mul(sc_s[sq][:], sc_s[sq][:], SCL))
        for sq in range(4):
            n = sq + 1
            if sq == 0:
                pass
            else:
                s3 = sc_s[sq][:].rearrange("p (h k) -> p h k", h=16)[:, :, 0:n]
                p.vsync(v.reduce_max(mbuf[:].rearrange("p (h o) -> p h o", o=1), s3, axis=AX.X))
                e3 = ebuf[:].rearrange("p (h k) -> p h k", h=16)[:, :, 0:n]
                p.vsync(v.tensor_tensor(e3, s3,
                                        mbuf[:].rearrange("p (h o) -> p h o", o=1)
                                        .to_broadcast((128, 16, n)), ALU.subtract))
                sv2 = p.sem("selfsync_v")
                s.wait_ge(sv2.h, sv2.v)
                lnx.inc(s.activation(ebuf[:], ebuf[:], AF.Exp), 1)
                v.wait_ge(lnx.h, lnx.v)
                e3b = ebuf[:].rearrange("p (h k) -> p h k", h=16)[:, :, 0:n]
                p.vsync(v.reduce_sum(zbuf[:].rearrange("p (h o) -> p h o", o=1), e3b, axis=AX.X))
                p.vsync(v.reciprocal(zbuf[:], zbuf[:]))
                p3 = pbuf[:].rearrange("p (h k) -> p h k", h=16)[:, :, 0:n]
                p.vsync(v.tensor_tensor(p3, e3b,
                                        zbuf[:].rearrange("p (h o) -> p h o", o=1)
                                        .to_broadcast((128, 16, n)), ALU.mult))
            if sq == 0:
                p.vsync(v.tensor_copy(ctx[0][:], qkvg[0][:, 4096:6144]))
            else:
                p0 = pbuf[:].rearrange("p (h k) -> p h k", h=16)[:, :, 0:1] \
                    .to_broadcast((128, 16, 128))
                p.vsync(v.tensor_tensor(
                    ctx[0][:].rearrange("p (h z) -> p h z", h=16),
                    qkvg[0][:, 4096:6144].rearrange("p (h z) -> p h z", h=16),
                    p0, ALU.mult))
                for sk in range(1, n):
                    pc = pbuf[:].rearrange("p (h k) -> p h k", h=16)[:, :, sk:sk + 1] \
                        .to_broadcast((128, 16, 128))
                    p.vsync(v.tensor_tensor(
                        ctmp[:].rearrange("p (h z) -> p h z", h=16),
                        qkvg[sk][:, 4096:6144].rearrange("p (h z) -> p h z", h=16),
                        pc, ALU.mult))
                    p.vsync(v.tensor_add(ctx[sk % 2][:], ctx[(sk - 1) % 2][:], ctmp[:]))
            src_ctx = ctx[0] if sq == 0 else ctx[(n - 1) % 2]
            p.vsync(v.tensor_copy(ctx_bf[:], src_ctx[:]))
            sv2 = p.sem("selfsync_v")
            t.wait_ge(sv2.h, sv2.v)
            tr = Ring(p, ps_t, "l1atr")
            for c2 in range(DCH):
                bank = tr.produce(t)
                tr.mm_done(t.transpose(bank[:], ctx_bf[:, c2 * 128:(c2 + 1) * 128], ident[:]))
                bb = tr.consume_wait(v)
                tr.ev_done(v.tensor_copy(ctxT[:, c2 * 128:(c2 + 1) * 128], bb[:]))
            t.wait_ge(tr.ev.h, tr.ev.v)
            tt = sq * 4 + grp
            hl.inc(y.dma_start(out=xt2[:], in_=x[tt * 128:(tt + 1) * 128, :]), 16)
            v.wait_ge(hl.h, hl.v)
            for ob in range(4):
                pairs = [(ctxT[:, c2 * 128:(c2 + 1) * 128],
                          prjs[:, c2 * DM + ob * 512:c2 * DM + (ob + 1) * 512])
                         for c2 in range(DCH)]
                pairs.append((ones_row[:], pbrow[:, ob * 512:(ob + 1) * 512]))
                mm.inc(mm_accum(t, ps_o[:], pairs), 1)
                v.wait_ge(mm.h, mm.v)
                v.wait_ge(sty.h, sty.v)
                ev.inc(v.tensor_add(outf[:], ps_o[:], xt2[:, ob * 512:(ob + 1) * 512]), 1)
                y.wait_ge(ev.h, ev.v)
                sty.inc(y.dma_start(out=xout[tt * 128:(tt + 1) * 128, ob * 512:(ob + 1) * 512],
                                    in_=outf[:]), 16)
                t.wait_ge(ev.h, ev.v)
    y.wait_ge(sty.h, sty.v)
    sc_c.close()
    p.barrier()
    p.compile_block()
    return p


def prep_l1_attn_a(inputs):
    base = dict(host_consts())
    base["wqkvT"] = np.ascontiguousarray(np.asarray(inputs["L1_attn_a_qkv_weight"], np.float32).T)
    base["qbias"] = np.asarray(inputs["L1_attn_a_qkv_bias"], np.float32).reshape(1, -1)
    base["lnw_c"] = _chunk_major(inputs["L1_attn_a_ln_weight"])
    base["lnb_c"] = _chunk_major(inputs["L1_attn_a_ln_bias"])
    base["projT"] = np.ascontiguousarray(np.asarray(inputs["L1_attn_a_proj_weight"], np.float32).T)
    base["pbias"] = np.asarray(inputs["L1_attn_a_proj_bias"], np.float32).reshape(1, -1)
    return base


def run_l1_attn_a(x_full, inputs, prog=None):
    if prog is None:
        prog = build_l1_attn_a()
    base = prep_l1_attn_a(inputs)
    in_maps = []
    for shard in shard_phaseB(x_full):
        m = dict(base); m["x"] = shard
        in_maps.append(m)
    res = run_bass_kernel_spmd(prog.nc, in_maps, core_ids=list(range(NCORES)))
    return unshard_phaseB([res.results[c]["xout"] for c in range(NCORES)])



def build_l0_attn_b():
    p = Prog()
    nc = p.nc
    NH = 16

    x = nc.dram_tensor("x", [TOK, DM], F32, kind="ExternalInput").ap()
    wqkv = nc.dram_tensor("wqkv", [DM, 3 * DM], F32, kind="ExternalInput").ap()
    anw_c = nc.dram_tensor("anw_c", [128, DCH], F32, kind="ExternalInput").ap()
    qnw = nc.dram_tensor("qnw", [1, 128], F32, kind="ExternalInput").ap()
    knw = nc.dram_tensor("knw", [1, 128], F32, kind="ExternalInput").ap()
    dw1r = nc.dram_tensor("dw1r", [DM, 256], F32, kind="ExternalInput").ap()
    qkwr = nc.dram_tensor("qkwr", [4, 64, 64], F32, kind="ExternalInput").ap()
    ddw = nc.dram_tensor("ddw", [DM, 64], F32, kind="ExternalInput").ap()
    wo = nc.dram_tensor("wo", [DM, DM], F32, kind="ExternalInput").ap()
    fnw_c = nc.dram_tensor("fnw_c", [128, DCH], F32, kind="ExternalInput").ap()
    w1f = nc.dram_tensor("w1f", [DM, 5632], F32, kind="ExternalInput").ap()
    w3f = nc.dram_tensor("w3f", [DM, 5632], F32, kind="ExternalInput").ap()
    w2f = nc.dram_tensor("w2f", [5632, DM], F32, kind="ExternalInput").ap()
    cosq = nc.dram_tensor("cosq", [TOK, 64], F32, kind="ExternalInput").ap()
    sinq = nc.dram_tensor("sinq", [TOK, 64], F32, kind="ExternalInput").ap()
    cosk = nc.dram_tensor("cosk", [TOK, 64], F32, kind="ExternalInput").ap()
    sink = nc.dram_tensor("sink", [TOK, 64], F32, kind="ExternalInput").ap()
    amask = nc.dram_tensor("amask", [NTT, 8, 128, 512], BF16, kind="ExternalInput").ap()
    xout = nc.dram_tensor("xout", [TOK, DM], F32, kind="ExternalOutput").ap()
    cst = decl_consts(nc)

    xln_scr = nc.dram_tensor("xln_scr", [TOK, DM], F32)
    q_scr = nc.dram_tensor("q_scr", [DM, TOK], BF16)
    qt0_scr = nc.dram_tensor("qt0_scr", [DM, TOK], BF16)
    qt1_scr = nc.dram_tensor("qt1_scr", [DM, TOK], BF16)
    kv_loc = nc.dram_tensor("kv_loc", [7 * TOK, DM], BF16)
    kv_all = nc.dram_tensor("kv_all", [14 * TOK, DM], BF16)
    dyn_loc = nc.dram_tensor("dyn_loc", [TOK, 320], F32)
    dyn_all = nc.dram_tensor("dyn_all", [2 * TOK, 320], F32)
    kdyn_loc = nc.dram_tensor("kdyn_loc", [48, TOK], BF16)
    kdyn_all = nc.dram_tensor("kdyn_all", [96, TOK], BF16)
    l_scrs = [nc.dram_tensor(f"l_scr{i}", [4, 8, 128, NH * 512], F32) for i in range(4)]
    def l_scr_ap(rt, cb):
        return l_scrs[rt // 4][rt % 4, cb]
    macc = nc.dram_tensor("macc", [TOK, DM], F32)
    h_scr = nc.dram_tensor("h_scr", [TOK, DM], F32)
    hid_scr = nc.dram_tensor("hid_scr", [NTT, 128, 5632], BF16)

    ident = nc.alloc_sbuf_tensor("ident", [128, 128], BF16)
    epsb = nc.alloc_sbuf_tensor("epsb", [128, 4], F32)
    ones_row = nc.alloc_sbuf_tensor("ones_row", [1, 128], BF16)
    wqb = nc.alloc_sbuf_tensor("wqb", [128, 128], BF16)
    wkb = nc.alloc_sbuf_tensor("wkb", [128, 128], BF16)
    ctabs = nc.alloc_sbuf_tensor("ctabs", [128, 4 * 64], F32)
    anwc = nc.alloc_sbuf_tensor("anwc", [128, DCH], F32)
    fnwc = nc.alloc_sbuf_tensor("fnwc", [128, DCH], F32)

    Mb = nc.alloc_sbuf_tensor("Mb", [128, NTT * 16], F32)
    Zb = nc.alloc_sbuf_tensor("Zb", [128, NTT * 16], F32)
    ld = p.sem("ld"); wl = p.sem("wl"); mm = p.sem("mm"); ev = p.sem("ev")
    sty = p.sem("sty"); stg = p.sem("stg"); cc = p.sem("cc"); hl = p.sem("hl")
    wlg = p.sem("wlg"); lnx = p.sem("lnx")
    y, t, v, s, g = (p.e[k] for k in ('y', 't', 'v', 's', 'g'))
    sv = p.sem("selfsync_v")

    ld.inc(y.dma_start(out=ident[:], in_=cst["ident"]), 16)
    ld.inc(y.dma_start(out=epsb[:], in_=cst["eps"]), 16)
    ld.inc(y.dma_start(out=ones_row[:], in_=cst["ones_row"]), 16)
    ld.inc(y.dma_start(out=anwc[:], in_=anw_c), 16)
    ld.inc(y.dma_start(out=fnwc[:], in_=fnw_c), 16)
    stg.inc(g.dma_start(out=xout, in_=x), 16)
    for e in (y, t, v, s, g):
        e.wait_ge(ld.h, ld.v)

    sc_0 = ScopeAlloc(nc)
    nw_f = sc_0.sb("nw_f", [1, 256], F32)
    nw_b = sc_0.sb("nw_b", [1, 256], BF16)
    ps_w = sc_0.ps("ps_w", [128, 128])
    ld.inc(y.dma_start(out=nw_f[:, 0:128], in_=qnw), 16)
    ld.inc(y.dma_start(out=nw_f[:, 128:256], in_=knw), 16)
    v.wait_ge(ld.h, ld.v)
    p.vsync(v.tensor_copy(nw_b[:], nw_f[:]))
    t.wait_ge(sv.h, sv.v)
    mm.inc(t.matmul(ps_w[:], ones_row[:], nw_b[:, 0:128], start=True, stop=True), 1)
    v.wait_ge(mm.h, mm.v)
    ev.inc(v.tensor_copy(wqb[:], ps_w[:]), 1)
    t.wait_ge(ev.h, ev.v)
    mm.inc(t.matmul(ps_w[:], ones_row[:], nw_b[:, 128:256], start=True, stop=True), 1)
    v.wait_ge(mm.h, mm.v)
    ev.inc(v.tensor_copy(wkb[:], ps_w[:]), 1)
    t.wait_ge(ev.h, ev.v)
    sc_0.close()

    sc_1 = ScopeAlloc(nc)
    l_xt = sc_1.sb("l_xt", [128, DM], F32)
    l_scrap = sc_1.sb("l_scrap", [128, DM], BF16)
    l_out = sc_1.sb("l_out", [128, DM], F32)
    l_st = sc_1.sb("l_st", [128, 8], F32)
    for tt in range(NTT):
        y.wait_ge(ev.h, ev.v)
        y.wait_ge(lnx.h, lnx.v)
        hl.inc(y.dma_start(out=l_xt[:], in_=x[tt * 128:(tt + 1) * 128, :]), 16)
        v.wait_ge(hl.h, hl.v)
        s.wait_ge(hl.h, hl.v)
        v.wait_ge(sty.h, sty.v)
        last = emit_ln_tile(p, l_xt, l_st, l_out, l_scrap, eps_ap=epsb[:, 0:1])
        ev.inc(last, 1)
        y.wait_ge(ev.h, ev.v)
        sty.inc(y.dma_start(out=xln_scr[tt * 128:(tt + 1) * 128, :], in_=l_out[:]), 16)
    sc_1.close()
    p.barrier()

    sc_a = ScopeAlloc(nc)
    hnT = sc_a.sb("hnT", [128, DCH * TOK], BF16)
    emit_ln_to_xnT(p, x, hnT, ident, epsb[:, 0:1], gainT_c=anwc,
                   rms2_eps=1e-6, rms2_bias=epsb[:, 1:2])

    sc_b = ScopeAlloc(nc)
    wq3 = sc_b.sb("wq3", [128, DCH * 512], BF16)
    dwb = sc_b.sb("dwb", [128, DCH * 256], BF16)
    ddb = sc_b.sb("ddb", [128, DCH * 64], BF16)
    qkwb = sc_b.sb("qkwb", [128, 4 * 64], BF16)
    qkvt = sc_b.sb("qkvt", [128, 3 * DM], F32)
    qrop = sc_b.sb("qrop", [128, 2 * DM], BF16)
    ropescr = sc_b.sb("ropescr", [128, DM], F32)
    dwh = sc_b.sb("dwh", [128, 256], F32)
    dwhb = sc_b.sb("dwhb", [128, 256], BF16)
    dwhT = sc_b.sb("dwhT", [128, 256], BF16)
    wt = sc_b.sb("wt", [128, 256], F32)
    wsc = sc_b.sb("wsc", [128, 256], BF16)
    ddv = sc_b.sb("ddv", [128, 64], F32)
    dynt = sc_b.sb("dynt", [128, 320], F32)
    nstat = sc_b.sb("nstat", [128, 16], F32)
    nstat2 = sc_b.sb("nstat2", [128, 16], F32)
    varb = sc_b.sb("varb", [128, DM], BF16)
    kdt = sc_b.sb("kdt", [128, 48], BF16)
    kdpad = sc_b.sb("kdpad", [128, 128], BF16)
    ps_m = [sc_b.ps(f"ps_m{i}", [128, 512]) for i in range(2)]
    ps_s64 = [sc_b.ps(f"ps_s64{i}", [128, 64]) for i in range(2)]
    ps_tr = [sc_b.ps(f"ps_tr{i}", [128, 128], BF16) for i in range(2)]
    m_ring = Ring(p, ps_m, "abq")

    for k in range(DCH):
        wlg.inc(g.dma_start(out=dwb[:, k * 256:(k + 1) * 256], in_=dw1r[k * 128:(k + 1) * 128, :]), 16)
        wlg.inc(g.dma_start(out=ddb[:, k * 64:(k + 1) * 64], in_=ddw[k * 128:(k + 1) * 128, :]), 16)
    wlg.inc(g.dma_start(out=qkwb[0:64, :].rearrange("p (c o) -> p c o", c=4),
                        in_=qkwr.rearrange("c k o -> k c o")), 16)
    wlg.inc(g.dma_start(out=qkwb[64:128, :].rearrange("p (c o) -> p c o", c=4),
                        in_=qkwr.rearrange("c k o -> k c o")), 16)
    t.wait_ge(wlg.h, wlg.v)

    def tile_rope_interleaved(dst, srcq, c_ap, s_ap, nheads):
        x1 = srcq.rearrange("p (h z) -> p h z", h=nheads)[:, :, 0:64]
        x2 = srcq.rearrange("p (h z) -> p h z", h=nheads)[:, :, 64:128]
        de = dst.rearrange("p (h z w) -> p h z w", h=nheads, w=2)[:, :, :, 0:1]
        do = dst.rearrange("p (h z w) -> p h z w", h=nheads, w=2)[:, :, :, 1:2]
        s1 = ropescr[:].rearrange("p (h z) -> p h z", h=16)[:, 0:nheads, 0:64]
        s2 = ropescr[:].rearrange("p (h z) -> p h z", h=16)[:, 0:nheads, 64:128]
        cb = c_ap.unsqueeze(1).to_broadcast((128, nheads, 64))
        sb2 = s_ap.unsqueeze(1).to_broadcast((128, nheads, 64))
        p.vsync(v.tensor_tensor(s1, x1, cb, ALU.mult))
        p.vsync(v.tensor_tensor(s2, x2, sb2, ALU.mult))
        p.vsync(v.tensor_tensor(de.squeeze(3), s1, s2, ALU.subtract))
        p.vsync(v.tensor_tensor(s1, x2, cb, ALU.mult))
        p.vsync(v.tensor_tensor(s2, x1, sb2, ALU.mult))
        p.vsync(v.tensor_tensor(do.squeeze(3), s1, s2, ALU.add))

    for tt in range(NTT):
        y.wait_ge(sv.h, sv.v)
        hl.inc(y.dma_start(out=ctabs[:, 0:64], in_=cosq[tt * 128:(tt + 1) * 128, :]), 16)
        hl.inc(y.dma_start(out=ctabs[:, 64:128], in_=sinq[tt * 128:(tt + 1) * 128, :]), 16)
        hl.inc(y.dma_start(out=ctabs[:, 128:192], in_=cosk[tt * 128:(tt + 1) * 128, :]), 16)
        hl.inc(y.dma_start(out=ctabs[:, 192:256], in_=sink[tt * 128:(tt + 1) * 128, :]), 16)
        hn_t = [hnT[:, k * TOK + tt * 128:k * TOK + (tt + 1) * 128] for k in range(DCH)]
        for ob in range(12):
            g.wait_ge(m_ring.mm.h, m_ring.mm.v)
            for k in range(DCH):
                wl.inc(g.dma_start(out=wq3[:, k * 512:(k + 1) * 512],
                                   in_=wqkv[k * 128:(k + 1) * 128, ob * 512:(ob + 1) * 512]), 16)
            t.wait_ge(wl.h, wl.v)
            bank = m_ring.produce(t)
            m_ring.mm_done(mm_accum(t, bank[:], [(hn_t[k], wq3[:, k * 512:(k + 1) * 512])
                                                 for k in range(DCH)]))
            bq = m_ring.consume_wait(v)
            m_ring.ev_done(v.tensor_copy(qkvt[:, ob * 512:(ob + 1) * 512], bq[:]))
        bank = m_ring.produce(t)
        m_ring.mm_done(mm_accum(t, bank[:, 0:256], [(hn_t[k], dwb[:, k * 256:(k + 1) * 256])
                                                    for k in range(DCH)]))
        bq = m_ring.consume_wait(s)
        m_ring.ev_done(s.activation(dwh[:], bq[:, 0:256], AF.Gelu))
        bank = m_ring.produce(t)
        m_ring.mm_done(mm_accum(t, bank[:, 0:64], [(hn_t[k], ddb[:, k * 64:(k + 1) * 64])
                                                   for k in range(DCH)]))
        bq = m_ring.consume_wait(s)
        m_ring.ev_done(s.activation(ddv[:], bq[:, 0:64], AF.Tanh))
        lnx.inc(s.engine_nop() if False else s.activation(dwhb[:], dwh[:], AF.Identity), 1)
        v.wait_ge(lnx.h, lnx.v)
        t.wait_ge(lnx.h, lnx.v)
        tr = Ring(p, ps_tr, "abtr")
        for blk in range(2):
            bank = tr.produce(t)
            tr.mm_done(t.transpose(bank[:], dwhb[:, blk * 128:(blk + 1) * 128], ident[:]))
            bb = tr.consume_wait(v)
            tr.ev_done(v.tensor_copy(dwhT[:, blk * 128:(blk + 1) * 128], bb[:]))
        t.wait_ge(tr.ev.h, tr.ev.v)
        s64r = Ring(p, ps_s64, "abs64")
        for c4 in range(4):
            blk = c4 // 2
            half = (c4 % 2) * 64
            bank = s64r.produce(t)
            s64r.mm_done(t.matmul(bank[:], dwhT[half:half + 64, blk * 128:(blk + 1) * 128],
                                  qkwb[half:half + 64, c4 * 64:(c4 + 1) * 64],
                                  start=True, stop=True))
            bb = s64r.consume_wait(v)
            s64r.ev_done(v.tensor_copy(wt[:, c4 * 64:(c4 + 1) * 64], bb[:]))
        v.wait_ge(s64r.ev.h, s64r.ev.v)
        s.wait_ge(s64r.ev.h, s64r.ev.v)
        lnx.inc(s.activation(wsc[:], wt[:], AF.Square), 1)
        v.wait_ge(lnx.h, lnx.v)
        p.vsync(v.reduce_sum(nstat[:].rearrange("p (a o) -> p a o", o=1),
                             wsc[:].rearrange("p (a n) -> p a n", n=16), axis=AX.X))
        p.vsync(v.tensor_scalar_mul(nstat[:], nstat[:], 1.0 / 16.0))
        lnx.inc(s.activation(nstat2[:], nstat[:], AF.Sqrt, bias=epsb[:, 1:2]), 1)
        v.wait_ge(lnx.h, lnx.v)
        p.vsync(v.reciprocal(nstat[:], nstat2[:]))
        for c4 in range(4):
            for ip in range(2):
                gidx = c4 * 4 + ip
                p.vsync(v.tensor_scalar_mul(wt[:, c4 * 64 + ip * 16:c4 * 64 + (ip + 1) * 16],
                                            wt[:, c4 * 64 + ip * 16:c4 * 64 + (ip + 1) * 16],
                                            nstat[:, gidx:gidx + 1]))
        p.vsync(v.tensor_copy(dynt[:, 0:256], wt[:]))
        p.vsync(v.tensor_copy(dynt[:, 256:320], ddv[:]))
        y.wait_ge(sv.h, sv.v)
        sty.inc(y.dma_start(out=dyn_loc[tt * 128:(tt + 1) * 128, :], in_=dynt[:]), 16)
        p.vsync(v.memset(kdpad[:], 0.0))
        p.vsync(v.tensor_copy(kdpad[:, 0:32], dynt[:, 96:128]))
        p.vsync(v.tensor_copy(kdpad[:, 32:48], dynt[:, 272:288]))
        t.wait_ge(sv.h, sv.v)
        tr2 = Ring(p, ps_tr, "abtr")
        bank = tr2.produce(t)
        tr2.mm_done(t.transpose(bank[:], kdpad[:], ident[:]))
        bb = tr2.consume_wait(v)
        tr2.ev_done(v.tensor_copy(kdt[:, 0:48] if False else kdpad[:, 0:48], bb[0:128, 0:48]) if False
                    else v.tensor_copy(kdpad[:], bb[:]))
        y.wait_ge(tr2.ev.h, tr2.ev.v)
        sty.inc(y.dma_start(out=kdyn_loc[:, tt * 128:(tt + 1) * 128],
                            in_=kdpad[0:48, :]), 16)
        v.wait_ge(hl.h, hl.v)
        s.wait_ge(m_ring.ev.h, m_ring.ev.v)
        s.wait_ge(sv.h, sv.v)
        lnx.inc(s.activation(ropescr[:], qkvt[:, 0:DM], AF.Square), 1)
        v.wait_ge(lnx.h, lnx.v)
        p.vsync(v.reduce_sum(nstat[:].rearrange("p (a o) -> p a o", o=1),
                             ropescr[:].rearrange("p (a n) -> p a n", n=128), axis=AX.X))
        p.vsync(v.tensor_scalar_mul(nstat[:], nstat[:], 1.0 / 128.0))
        lnx.inc(s.activation(nstat2[:], nstat[:], AF.Sqrt, bias=epsb[:, 1:2]), 1)
        v.wait_ge(lnx.h, lnx.v)
        p.vsync(v.reciprocal(nstat[:], nstat2[:]))
        for nh in range(NH):
            p.vsync(v.scalar_tensor_tensor(qkvt[:, nh * 128:(nh + 1) * 128],
                                           qkvt[:, nh * 128:(nh + 1) * 128],
                                           nstat[:, nh:nh + 1], wqb[:],
                                           ALU.mult, ALU.mult))
        lnx.inc(s.activation(ropescr[:], qkvt[:, DM:2 * DM], AF.Square), 1)
        v.wait_ge(lnx.h, lnx.v)
        p.vsync(v.reduce_sum(nstat[:].rearrange("p (a o) -> p a o", o=1),
                             ropescr[:].rearrange("p (a n) -> p a n", n=128), axis=AX.X))
        p.vsync(v.tensor_scalar_mul(nstat[:], nstat[:], 1.0 / 128.0))
        lnx.inc(s.activation(nstat2[:], nstat[:], AF.Sqrt, bias=epsb[:, 1:2]), 1)
        v.wait_ge(lnx.h, lnx.v)
        p.vsync(v.reciprocal(nstat[:], nstat2[:]))
        for nh in range(NH):
            p.vsync(v.scalar_tensor_tensor(qkvt[:, DM + nh * 128:DM + (nh + 1) * 128],
                                           qkvt[:, DM + nh * 128:DM + (nh + 1) * 128],
                                           nstat[:, nh:nh + 1], wkb[:],
                                           ALU.mult, ALU.mult))
        tile_rope_interleaved(qrop[:, 0:DM], qkvt[:, 0:DM], ctabs[:, 0:64], ctabs[:, 64:128], NH)
        tile_rope_interleaved(qrop[:, DM:2 * DM], qkvt[:, DM:2 * DM],
                              ctabs[:, 128:192], ctabs[:, 192:256], NH)
        g.wait_ge(sv.h, sv.v)
        stg.inc(g.dma_start(out=kv_loc[3 * TOK + tt * 128:3 * TOK + (tt + 1) * 128, :],
                            in_=qkvt[:, 2 * DM:3 * DM]), 16)
        for i2 in range(2):
            for nh in range(NH):
                p.vsync(v.tensor_scalar_mul(varb[:, nh * 128:(nh + 1) * 128],
                                            qrop[:, nh * 128:(nh + 1) * 128],
                                            dynt[:, i2 * 16 + nh:i2 * 16 + nh + 1]))
            tr3 = Ring(p, ps_tr, "abtr")
            t.wait_ge(sv.h, sv.v)
            for c2 in range(DCH):
                bank = tr3.produce(t)
                tr3.mm_done(t.transpose(bank[:], varb[:, c2 * 128:(c2 + 1) * 128], ident[:]))
                bb = tr3.consume_wait(v)
                tr3.ev_done(v.tensor_copy(ropescr[:, c2 * 128:(c2 + 1) * 128].bitcast(BF16)[:, 0:128], bb[:]) if False
                            else v.tensor_copy(varb[:, c2 * 128:(c2 + 1) * 128], bb[:]))
            y.wait_ge(tr3.ev.h, tr3.ev.v)
            dst = qt0_scr if i2 == 0 else qt1_scr
            sty.inc(y.dma_start(out=dst[:, tt * 128:(tt + 1) * 128]
                                .rearrange("(c p) q -> p c q", p=128),
                                in_=varb[:].rearrange("p (c q) -> p c q", c=DCH)), 16)
            for nh in range(NH):
                p.vsync(v.tensor_scalar_mul(varb[:, nh * 128:(nh + 1) * 128],
                                            qrop[:, DM + nh * 128:DM + (nh + 1) * 128],
                                            dynt[:, 64 + i2 * 16 + nh:64 + i2 * 16 + nh + 1]))
            tr3 = Ring(p, ps_tr, "abtr")
            t.wait_ge(sv.h, sv.v)
            for c2 in range(DCH):
                bank = tr3.produce(t)
                tr3.mm_done(t.transpose(bank[:], varb[:, c2 * 128:(c2 + 1) * 128], ident[:]))
                bb = tr3.consume_wait(v)
                tr3.ev_done(v.tensor_copy(varb[:, c2 * 128:(c2 + 1) * 128], bb[:]))
            y.wait_ge(tr3.ev.h, tr3.ev.v)
            sty.inc(y.dma_start(out=kv_loc[(1 + i2) * TOK + tt * 128:(1 + i2) * TOK + (tt + 1) * 128, :]
                                .rearrange("q (c p) -> p c q", p=128) if False else
                                kv_loc[(1 + i2) * TOK:(2 + i2) * TOK, :]
                                .rearrange("q d -> q d")[0:1, 0:1] if False else
                                kv_loc[(1 + i2) * TOK + 0:(1 + i2) * TOK + TOK, :][0:1, 0:1],
                                in_=varb[0:1, 0:1]), 16) if False else None
            sty.inc(y.dma_start(out=kv_loc[(1 + i2) * TOK:(1 + i2) * TOK + DM,
                                           tt * 128:(tt + 1) * 128]
                                .rearrange("(c p) q -> p c q", p=128),
                                in_=varb[:].rearrange("p (c q) -> p c q", c=DCH)), 16)
            for nh in range(NH):
                p.vsync(v.tensor_scalar_mul(varb[:, nh * 128:(nh + 1) * 128],
                                            qkvt[:, 2 * DM + nh * 128:2 * DM + (nh + 1) * 128],
                                            dynt[:, 224 + i2 * 16 + nh:224 + i2 * 16 + nh + 1]))
            y.wait_ge(sv.h, sv.v)
            sty.inc(y.dma_start(out=kv_loc[(4 + i2) * TOK + tt * 128:(4 + i2) * TOK + (tt + 1) * 128, :],
                                in_=varb[:]), 16)
        for nh in range(NH):
            p.vsync(v.tensor_scalar_mul(varb[:, nh * 128:(nh + 1) * 128],
                                        qkvt[:, 2 * DM + nh * 128:2 * DM + (nh + 1) * 128],
                                        dynt[:, 304 + nh:304 + nh + 1]))
        y.wait_ge(sv.h, sv.v)
        sty.inc(y.dma_start(out=kv_loc[6 * TOK + tt * 128:6 * TOK + (tt + 1) * 128, :],
                            in_=varb[:]), 16)
        p.vsync(v.tensor_copy(varb[:], qrop[:, DM:2 * DM]))
        tr4 = Ring(p, ps_tr, "abtr")
        t.wait_ge(sv.h, sv.v)
        for c2 in range(DCH):
            bank = tr4.produce(t)
            tr4.mm_done(t.transpose(bank[:], varb[:, c2 * 128:(c2 + 1) * 128], ident[:]))
            bb = tr4.consume_wait(v)
            tr4.ev_done(v.tensor_copy(varb[:, c2 * 128:(c2 + 1) * 128], bb[:]))
        y.wait_ge(tr4.ev.h, tr4.ev.v)
        sty.inc(y.dma_start(out=kv_loc[0:DM, tt * 128:(tt + 1) * 128]
                            .rearrange("(c p) q -> p c q", p=128),
                            in_=varb[:].rearrange("p (c q) -> p c q", c=DCH)), 16)
        p.vsync(v.tensor_copy(varb[:], qrop[:, 0:DM]))
        tr5 = Ring(p, ps_tr, "abtr")
        t.wait_ge(sv.h, sv.v)
        for c2 in range(DCH):
            bank = tr5.produce(t)
            tr5.mm_done(t.transpose(bank[:], varb[:, c2 * 128:(c2 + 1) * 128], ident[:]))
            bb = tr5.consume_wait(v)
            tr5.ev_done(v.tensor_copy(varb[:, c2 * 128:(c2 + 1) * 128], bb[:]))
        y.wait_ge(tr5.ev.h, tr5.ev.v)
        sty.inc(y.dma_start(out=q_scr[:, tt * 128:(tt + 1) * 128]
                            .rearrange("(c p) q -> p c q", p=128),
                            in_=varb[:].rearrange("p (c q) -> p c q", c=DCH)), 16)
        v.wait_ge(sty.h, sty.v)
        v.wait_ge(stg.h, stg.v)
    sc_b.close()
    sc_a.close()
    p.barrier()

    g.wait_ge(sty.h, sty.v)
    cc.inc(g.collective_compute("AllGather", ALU.bypass,
                                replica_groups=[[0, 1], [2, 3], [4, 5], [6, 7]],
                                ins=[kv_loc[:]], outs=[kv_all[:]]), 1)
    cc.inc(g.collective_compute("AllGather", ALU.bypass,
                                replica_groups=[[0, 1], [2, 3], [4, 5], [6, 7]],
                                ins=[dyn_loc[:]], outs=[dyn_all[:]]), 1)
    cc.inc(g.collective_compute("AllGather", ALU.bypass,
                                replica_groups=[[0, 1], [2, 3], [4, 5], [6, 7]],
                                ins=[kdyn_loc[:]], outs=[kdyn_all[:]]), 1)
    for e in (y, t, v, s, g):
        e.wait_ge(cc.h, cc.v)

    CBMAX = [min(8, (2048 + (rt + 1) * 128 + 511) // 512) for rt in range(NTT)]

    sc_p = ScopeAlloc(nc)
    kblk = sc_p.sb("kblk", [128, DCH * 512], BF16)
    ktblk = [sc_p.sb(f"ktblk{i}", [128, DCH * 512], BF16) for i in range(2)]
    bcast = sc_p.sb("bcast", [128, 48 * 512], BF16)
    kdr1 = sc_p.sb("kdr1", [1, 512], BF16)
    qblk = sc_p.sb("qblk", [128, DCH * 128], BF16)
    qtblk = [sc_p.sb(f"qtblk{i}", [128, DCH * 128], BF16) for i in range(2)]
    dynq = sc_p.sb("dynq", [128, 320], F32)
    qdd1 = sc_p.sb("qdd1", [128, 16], F32)
    hbb = sc_p.sb("hbb", [128, 4 * 512], F32)
    l_all = sc_p.sb("l_all", [128, NH * 512], F32)
    ltmp = sc_p.sb("ltmp", [128, 2 * 512], F32)
    mrow = sc_p.sb("mrow", [128, 512], BF16)
    st16 = sc_p.sb("st16", [128, 5 * 16], F32)
    ebuf1 = sc_p.sb("ebuf1", [128, 512], BF16)
    ps_h = [sc_p.ps(f"ps_h{i}", [128, 512]) for i in range(2)]
    ps_l = [sc_p.ps(f"ps_l{i}", [128, 512]) for i in range(2)]
    ps_bc = [sc_p.ps(f"ps_bc{i}", [128, 512]) for i in range(2)]
    h_ring = Ring(p, ps_h, "abh")
    l_ring = Ring(p, ps_l, "abl")
    bc_ring = Ring(p, ps_bc, "abbc")

    for cb in range(8):
        r = (cb * 512) // TOK
        cl = (cb * 512) % TOK
        base_k = r * 7 * TOK
        y.wait_ge(h_ring.mm.h, h_ring.mm.v)
        y.wait_ge(l_ring.mm.h, l_ring.mm.v)
        for k in range(DCH):
            hl.inc(y.dma_start(out=kblk[:, k * 512:(k + 1) * 512],
                               in_=kv_all[base_k + k * 128:base_k + k * 128 + 128, cl:cl + 512]), 16)
            for i2 in range(2):
                hl.inc(y.dma_start(out=ktblk[i2][:, k * 512:(k + 1) * 512],
                                   in_=kv_all[base_k + (1 + i2) * TOK + k * 128:
                                              base_k + (1 + i2) * TOK + k * 128 + 128,
                                              cl:cl + 512]), 16)
        t.wait_ge(hl.h, hl.v)
        kdl = p.sem("kdl")
        for j in range(48):
            if j >= 1:
                y.wait_ge(bc_ring.mm.h, bc_ring.mm.v)
            kdl.inc(y.dma_start(out=kdr1[:], in_=kdyn_all[r * 48 + j:r * 48 + j + 1,
                                                          cl:cl + 512]), 16)
            t.wait_ge(kdl.h, kdl.v)
            bank = bc_ring.produce(t)
            bc_ring.mm_done(t.matmul(bank[:], ones_row[:], kdr1[:],
                                     start=True, stop=True))
            bb = bc_ring.consume_wait(v)
            bc_ring.ev_done(v.tensor_copy(bcast[:, j * 512:(j + 1) * 512], bb[:]))
        for rt in range(NTT):
            if cb >= CBMAX[rt]:
                continue
            y.wait_ge(h_ring.mm.h, h_ring.mm.v)
            y.wait_ge(l_ring.mm.h, l_ring.mm.v)
            y.wait_ge(sv.h, sv.v)
            hl.inc(y.dma_start(out=qblk[:].rearrange("p (c q) -> p c q", c=DCH),
                               in_=q_scr[:, rt * 128:(rt + 1) * 128]
                               .rearrange("(c p) q -> p c q", p=128)), 16)
            for i2, scr2 in ((0, qt0_scr), (1, qt1_scr)):
                hl.inc(y.dma_start(out=qtblk[i2][:].rearrange("p (c q) -> p c q", c=DCH),
                                   in_=scr2[:, rt * 128:(rt + 1) * 128]
                                   .rearrange("(c p) q -> p c q", p=128)), 16)
            hl.inc(y.dma_start(out=dynq[:], in_=dyn_loc[rt * 128:(rt + 1) * 128, :]), 16)
            hl.inc(y.dma_start(out=mrow[:], in_=amask[rt, cb]), 16)
            v.wait_ge(hl.h, hl.v)
            p.vsync(v.tensor_scalar_add(qdd1[:], dynq[:, 256:272], 1.0))
            t.wait_ge(hl.h, hl.v)
            for i2 in range(2):
                bank = h_ring.produce(t)
                h_ring.mm_done(mm_accum(t, bank[:], [(qtblk[i2][:, k * 128:(k + 1) * 128],
                                                      kblk[:, k * 512:(k + 1) * 512])
                                                     for k in range(DCH)]))
                bb = h_ring.consume_wait(v)
                h_ring.ev_done(v.tensor_copy(hbb[:, i2 * 512:(i2 + 1) * 512], bb[:]))
            for i2 in range(2):
                bank = h_ring.produce(t)
                h_ring.mm_done(mm_accum(t, bank[:], [(qblk[:, k * 128:(k + 1) * 128],
                                                      ktblk[i2][:, k * 512:(k + 1) * 512])
                                                     for k in range(DCH)]))
                bb = h_ring.consume_wait(v)
                h_ring.ev_done(v.tensor_copy(hbb[:, (2 + i2) * 512:(3 + i2) * 512], bb[:]))
            v.wait_ge(h_ring.ev.h, h_ring.ev.v)
            for nh in range(NH):
                bank = l_ring.produce(t)
                l_ring.mm_done(t.matmul(bank[:], qblk[:, nh * 128:(nh + 1) * 128],
                                        kblk[:, nh * 512:(nh + 1) * 512],
                                        start=True, stop=True))
                bb = l_ring.consume_wait(v)
                p.vsync(v.tensor_mul(ltmp[:, 0:512], bb[:], bcast[:, (32 + nh) * 512:(33 + nh) * 512]))
                p.vsync(v.scalar_tensor_tensor(ltmp[:, 512:1024], bb[:], qdd1[:, nh:nh + 1],
                                               ltmp[:, 0:512], ALU.mult, ALU.add))
                l_ring.ev_done(v.engine_nop())
                p.vsync(v.scalar_tensor_tensor(ltmp[:, 0:512], hbb[:, 0:512],
                                               dynq[:, 32 + nh:32 + nh + 1],
                                               ltmp[:, 512:1024], ALU.mult, ALU.add))
                p.vsync(v.scalar_tensor_tensor(ltmp[:, 512:1024], hbb[:, 512:1024],
                                               dynq[:, 48 + nh:48 + nh + 1],
                                               ltmp[:, 0:512], ALU.mult, ALU.add))
                p.vsync(v.tensor_mul(ltmp[:, 0:512], hbb[:, 1024:1536],
                                     bcast[:, nh * 512:(nh + 1) * 512]))
                p.vsync(v.tensor_add(ltmp[:, 512:1024], ltmp[:, 512:1024], ltmp[:, 0:512]))
                p.vsync(v.tensor_mul(ltmp[:, 0:512], hbb[:, 1536:2048],
                                     bcast[:, (16 + nh) * 512:(17 + nh) * 512]))
                p.vsync(v.tensor_add(ltmp[:, 512:1024], ltmp[:, 512:1024], ltmp[:, 0:512]))
                p.vsync(v.tensor_add(l_all[:, nh * 512:(nh + 1) * 512],
                                     ltmp[:, 512:1024], mrow[:]))
            l3 = l_all[:].rearrange("p (n s) -> p n s", n=NH)
            p.vsync(v.reduce_max(st16[:, 0:16].rearrange("p (a o) -> p a o", o=1), l3, axis=AX.X))
            Ms = Mb[:, rt * 16:(rt + 1) * 16]
            Zs = Zb[:, rt * 16:(rt + 1) * 16]
            if cb == 0:
                p.vsync(v.tensor_copy(st16[:, 16:32], st16[:, 0:16]))
            else:
                p.vsync(v.tensor_max(st16[:, 16:32], Ms, st16[:, 0:16]))
            p.vsync(v.tensor_scalar_mul(st16[:, 64:80], st16[:, 16:32], -1.0))
            s.wait_ge(sv.h, sv.v)
            for nh in range(NH):
                lnx.inc(s.activation(ebuf1[:], l_all[:, nh * 512:(nh + 1) * 512], AF.Exp,
                                     bias=st16[:, 64 + nh:64 + nh + 1]), 1)
                v.wait_ge(lnx.h, lnx.v)
                p.vsync(v.reduce_sum(st16[:, 32 + nh:32 + nh + 1], ebuf1[:], axis=AX.X))
                s.wait_ge(sv.h, sv.v)
            if cb == 0:
                p.vsync(v.tensor_copy(Zs, st16[:, 32:48]))
                p.vsync(v.tensor_copy(Ms, st16[:, 16:32]))
            else:
                p.vsync(v.tensor_sub(st16[:, 0:16], Ms, st16[:, 16:32]))
                s.wait_ge(sv.h, sv.v)
                lnx.inc(s.activation(st16[:, 48:64], st16[:, 0:16], AF.Exp), 1)
                v.wait_ge(lnx.h, lnx.v)
                p.vsync(v.tensor_mul(Zs, Zs, st16[:, 48:64]))
                p.vsync(v.tensor_add(Zs, Zs, st16[:, 32:48]))
                p.vsync(v.tensor_copy(Ms, st16[:, 16:32]))
            y.wait_ge(sv.h, sv.v)
            sty.inc(y.dma_start(out=l_scr_ap(rt, cb), in_=l_all[:]), 16)
    s.wait_ge(sv.h, sv.v)
    lnz = sc_p.sb("lnz", [128, NTT * 16], F32)
    lnx.inc(s.activation(lnz[:], Zb[:], AF.Ln), 1)
    v.wait_ge(lnx.h, lnx.v)
    p.vsync(v.tensor_add(Mb[:], Mb[:], lnz[:]))
    p.vsync(v.tensor_scalar_mul(Mb[:], Mb[:], -1.0))
    sc_p.close()
    p.barrier()
    sc_q = ScopeAlloc(nc)
    vblk = sc_q.sb("vblk", [128, 4 * DM], BF16)
    vtblk = [sc_q.sb(f"vtblk{i}", [128, 4 * DM], BF16) for i in range(2)]
    vkblk = sc_q.sb("vkblk", [128, 4 * DM], BF16)
    l_all2 = sc_q.sb("l_all2", [128, NH * 512], F32)
    Pb = sc_q.sb("Pb", [128, NH * 512], BF16)
    PT = sc_q.sb("PT", [128, 4 * 128], BF16)
    dynq2 = sc_q.sb("dynq2", [128, 320], F32)
    qdd2 = sc_q.sb("qdd2", [128, 16], F32)
    dynk = sc_q.sb("dynk", [128, 4 * 32], F32)
    hp = [[sc_q.sb(f"hp{i}_{j}", [128, 512], F32) for j in range(2)] for i in range(2)]
    hpk = [[sc_q.sb(f"hpk{i}_{j}", [128, 4 * 128], F32) for j in range(2)] for i in range(2)]
    hp_bf = sc_q.sb("hp_bf", [128, 512], BF16)
    hpT_bf = sc_q.sb("hpT_bf", [128, 4 * 128], BF16)
    out_sb = sc_q.sb("out_sb", [128, DM], F32)
    gtmp = sc_q.sb("gtmp", [128, 512], F32)
    kdtmp = sc_q.sb("kdtmp", [128, 128], F32)
    ps_pt = [sc_q.ps(f"ps_pt{i}", [128, 128], BF16) for i in range(2)]
    ps_pv = sc_q.ps("ps_pv", [128, 128])
    ps_kd = sc_q.ps("ps_kd", [128, 128])
    ps_g2 = [sc_q.ps(f"ps_g2{i}", [128, 512]) for i in range(2)]
    g2_ring = Ring(p, ps_g2, "abg2")
    stga = p.sem("stga")

    for cb in range(8):
        r = (cb * 512) // TOK
        cl = (cb * 512) % TOK
        base_k = r * 7 * TOK
        y.wait_ge(mm.h, mm.v)
        for (buf, blk) in ((vblk, 3), (vtblk[0], 4), (vtblk[1], 5), (vkblk, 6)):
            hl.inc(y.dma_start(out=buf[:].rearrange("p (n d) -> p n d", n=4),
                               in_=kv_all[base_k + blk * TOK + cl:base_k + blk * TOK + cl + 512, :]
                               .rearrange("(n p) d -> p n d", p=128)), 16)
        hl.inc(y.dma_start(out=dynk[:].rearrange("p (n c) -> p n c", n=4),
                           in_=dyn_all[r * TOK + cl:r * TOK + cl + 512, 192:224]
                           .rearrange("(n p) c -> p n c", p=128)), 16)
        for rt in range(NTT):
            if cb >= CBMAX[rt]:
                continue
            y.wait_ge(mm.h, mm.v)
            y.wait_ge(sv.h, sv.v)
            hl.inc(y.dma_start(out=l_all2[:], in_=l_scr_ap(rt, cb)), 16)
            hl.inc(y.dma_start(out=dynq2[:], in_=dyn_loc[rt * 128:(rt + 1) * 128, :]), 16)
            v.wait_ge(hl.h, hl.v)
            s.wait_ge(hl.h, hl.v)
            p.vsync(v.tensor_scalar_add(qdd2[:], dynq2[:, 288:304], 1.0))
            for nh in range(NH):
                lnx.inc(s.activation(Pb[:, nh * 512:(nh + 1) * 512],
                                     l_all2[:, nh * 512:(nh + 1) * 512], AF.Exp,
                                     bias=Mb[:, rt * 16 + nh:rt * 16 + nh + 1]), 1)
            v.wait_ge(lnx.h, lnx.v)
            t.wait_ge(lnx.h, lnx.v)
            for nh in range(NH):
                trp = Ring(p, ps_pt, "abpt")
                for s4 in range(4):
                    bank = trp.produce(t)
                    trp.mm_done(t.transpose(bank[:], Pb[:, nh * 512 + s4 * 128:nh * 512 + (s4 + 1) * 128], ident[:]))
                    bb = trp.consume_wait(v)
                    trp.ev_done(v.tensor_copy(PT[:, s4 * 128:(s4 + 1) * 128], bb[:]))
                t.wait_ge(trp.ev.h, trp.ev.v)
                lastm = mm_accum(t, ps_pv[:], [(PT[:, s4 * 128:(s4 + 1) * 128],
                                                vblk[:, s4 * DM + nh * 128:s4 * DM + (nh + 1) * 128])
                                               for s4 in range(4)])
                mm.inc(lastm, 1)
                lastm = mm_accum(t, ps_kd[:], [(PT[:, s4 * 128:(s4 + 1) * 128],
                                                vkblk[:, s4 * DM + nh * 128:s4 * DM + (nh + 1) * 128])
                                               for s4 in range(4)])
                mm.inc(lastm, 1)
                v.wait_ge(mm.h, mm.v)
                p.vsync(v.tensor_copy(kdtmp[:], ps_kd[:]))
                p.vsync(v.scalar_tensor_tensor(out_sb[:, nh * 128:(nh + 1) * 128],
                                               ps_pv[:], qdd2[:, nh:nh + 1], kdtmp[:],
                                               ALU.mult, ALU.add))
                t.wait_ge(sv.h, sv.v)
                for i2 in range(2):
                    col = dynq2[:, 128 + i2 * 16 + nh:128 + i2 * 16 + nh + 1]
                    if nh == 0:
                        p.vsync(v.tensor_scalar_mul(hp[i2][0][:], Pb[:, 0:512], col))
                    else:
                        p.vsync(v.scalar_tensor_tensor(hp[i2][nh % 2][:],
                                                       Pb[:, nh * 512:(nh + 1) * 512], col,
                                                       hp[i2][(nh - 1) % 2][:], ALU.mult, ALU.add))
                for i2 in range(2):
                    for s4 in range(4):
                        col = dynk[:, s4 * 32 + i2 * 16 + nh:s4 * 32 + i2 * 16 + nh + 1]
                        dst = hpk[i2][nh % 2][:, s4 * 128:(s4 + 1) * 128]
                        if nh == 0:
                            p.vsync(v.tensor_scalar_mul(dst, PT[:, s4 * 128:(s4 + 1) * 128], col))
                        else:
                            p.vsync(v.scalar_tensor_tensor(dst, PT[:, s4 * 128:(s4 + 1) * 128], col,
                                                           hpk[i2][(nh - 1) % 2][:, s4 * 128:(s4 + 1) * 128],
                                                           ALU.mult, ALU.add))
            g.wait_ge(sv.h, sv.v)
            if cb == 0:
                stga.inc(g.dma_start(out=macc[rt * 128:(rt + 1) * 128, :], in_=out_sb[:]), 16)
                g.wait_ge(stga.h, stga.v)
            else:
                stga.inc(g.dma_start(out=macc[rt * 128:(rt + 1) * 128, :], in_=out_sb[:],
                                     accum_op=ALU.add), 16)
            last_pp = (NH - 1) % 2
            for i2 in range(2):
                p.vsync(v.tensor_copy(hp_bf[:], hp[i2][last_pp][:]))
                t.wait_ge(sv.h, sv.v)
                trp = Ring(p, ps_pt, "abpt")
                for s4 in range(4):
                    bank = trp.produce(t)
                    trp.mm_done(t.transpose(bank[:], hp_bf[:, s4 * 128:(s4 + 1) * 128], ident[:]))
                    bb = trp.consume_wait(v)
                    trp.ev_done(v.tensor_copy(hpT_bf[:, s4 * 128:(s4 + 1) * 128], bb[:]))
                t.wait_ge(trp.ev.h, trp.ev.v)
                for eb in range(4):
                    bank = g2_ring.produce(t)
                    g2_ring.mm_done(mm_accum(t, bank[:], [(hpT_bf[:, s4 * 128:(s4 + 1) * 128],
                                                           vblk[:, s4 * DM + eb * 512:s4 * DM + (eb + 1) * 512])
                                                          for s4 in range(4)]))
                    bb = g2_ring.consume_wait(v)
                    v.wait_ge(stga.h, stga.v)
                    for h4 in range(4):
                        nh2 = eb * 4 + h4
                        is_last = h4 == 3
                        op = v.tensor_scalar_mul(gtmp[:, h4 * 128:(h4 + 1) * 128],
                                                 bb[:, h4 * 128:(h4 + 1) * 128],
                                                 dynq2[:, 160 + i2 * 16 + nh2:160 + i2 * 16 + nh2 + 1])
                        p.vsync(op)
                    g2_ring.ev_done(v.engine_nop())
                    g.wait_ge(g2_ring.ev.h, g2_ring.ev.v)
                    stga.inc(g.dma_start(out=macc[rt * 128:(rt + 1) * 128, eb * 512:(eb + 1) * 512],
                                         in_=gtmp[:], accum_op=ALU.add), 16)
                p.vsync(v.tensor_copy(hpT_bf[:], hpk[i2][last_pp][:]))
                t.wait_ge(sv.h, sv.v)
                for eb in range(4):
                    bank = g2_ring.produce(t)
                    g2_ring.mm_done(mm_accum(t, bank[:], [(hpT_bf[:, s4 * 128:(s4 + 1) * 128],
                                                           vtblk[i2][:, s4 * DM + eb * 512:s4 * DM + (eb + 1) * 512])
                                                          for s4 in range(4)]))
                    bb = g2_ring.consume_wait(v)
                    v.wait_ge(stga.h, stga.v)
                    p.vsync(v.tensor_copy(gtmp[:], bb[:]))
                    g2_ring.ev_done(v.engine_nop())
                    g.wait_ge(g2_ring.ev.h, g2_ring.ev.v)
                    stga.inc(g.dma_start(out=macc[rt * 128:(rt + 1) * 128, eb * 512:(eb + 1) * 512],
                                         in_=gtmp[:], accum_op=ALU.add), 16)
    g.wait_ge(stga.h, stga.v)
    sc_q.close()
    p.barrier()

    sc_r = ScopeAlloc(nc)
    wos = sc_r.sb("wos", [128, DCH * DM], BF16)
    ot2 = sc_r.sb("ot2", [128, DM], BF16)
    oT2 = sc_r.sb("oT2", [128, DCH * 128], BF16)
    of32 = sc_r.sb("of32", [128, DM], F32)
    xlnt = sc_r.sb("xlnt", [128, DM], F32)
    hrow = sc_r.sb("hrow", [128, 512], F32)
    ps_tr5 = [sc_r.ps(f"ps_tr5{i}", [128, 128], BF16) for i in range(2)]
    ps_wo = sc_r.ps("ps_wo", [128, 512])
    for k in range(DCH):
        wlg.inc(g.dma_start(out=wos[:, k * DM:(k + 1) * DM], in_=wo[k * 128:(k + 1) * 128, :]), 16)
    t.wait_ge(wlg.h, wlg.v)
    for rt in range(NTT):
        y.wait_ge(sv.h, sv.v)
        y.wait_ge(mm.h, mm.v)
        hl.inc(y.dma_start(out=of32[:], in_=macc[rt * 128:(rt + 1) * 128, :]), 16)
        hl.inc(y.dma_start(out=xlnt[:], in_=xln_scr[rt * 128:(rt + 1) * 128, :]), 16)
        v.wait_ge(hl.h, hl.v)
        p.vsync(v.tensor_copy(ot2[:], of32[:]))
        t.wait_ge(sv.h, sv.v)
        trr = Ring(p, ps_tr5, "abtr5")
        for c2 in range(DCH):
            bank = trr.produce(t)
            trr.mm_done(t.transpose(bank[:], ot2[:, c2 * 128:(c2 + 1) * 128], ident[:]))
            bb = trr.consume_wait(v)
            trr.ev_done(v.tensor_copy(oT2[:, c2 * 128:(c2 + 1) * 128], bb[:]))
        t.wait_ge(trr.ev.h, trr.ev.v)
        for ob in range(4):
            lastm = mm_accum(t, ps_wo[:], [(oT2[:, c2 * 128:(c2 + 1) * 128],
                                            wos[:, c2 * DM + ob * 512:c2 * DM + (ob + 1) * 512])
                                           for c2 in range(DCH)])
            mm.inc(lastm, 1)
            v.wait_ge(mm.h, mm.v)
            v.wait_ge(sty.h, sty.v)
            v.wait_ge(stg.h, stg.v)
            ev.inc(v.tensor_add(hrow[:], ps_wo[:], xlnt[:, ob * 512:(ob + 1) * 512]), 1)
            y.wait_ge(ev.h, ev.v)
            g.wait_ge(ev.h, ev.v)
            sty.inc(y.dma_start(out=h_scr[rt * 128:(rt + 1) * 128, ob * 512:(ob + 1) * 512],
                                in_=hrow[:]), 16)
            stg.inc(g.dma_start(out=xout[rt * 128:(rt + 1) * 128, ob * 512:(ob + 1) * 512],
                                in_=hrow[:], accum_op=ALU.add), 16)
            t.wait_ge(ev.h, ev.v)
    y.wait_ge(sty.h, sty.v)
    g.wait_ge(stg.h, stg.v)
    sc_r.close()
    p.barrier()

    sc_f = ScopeAlloc(nc)
    fnT = sc_f.sb("fnT", [128, DCH * TOK], BF16)
    emit_ln_to_xnT(p, h_scr, fnT, ident, epsb[:, 1:2], gainT_c=fnwc,
                   rms_only=True, tag="ffnln")
    sc_f1 = ScopeAlloc(nc)
    w13 = [sc_f1.sb(f"w13_{i}", [128, DCH * 1024], BF16) for i in range(2)]
    gu2 = sc_f1.sb("gu2", [128, 2 * TOK], F32)
    hid2 = sc_f1.sb("hid2", [128, TOK], BF16)
    ps_f = [sc_f1.ps(f"ps_f{i}", [128, 512]) for i in range(2)]
    ps_f2 = [sc_f1.ps(f"ps_f2{i}", [128, 512]) for i in range(2)]
    f_ring = Ring(p, ps_f, "abf1")
    f2_ring = Ring(p, ps_f2, "abf2")
    NSUP2 = 11
    for sup in range(NSUP2):
        wb2 = w13[sup % 2]
        if sup >= 2:
            g.wait_ge(f2_ring.mm.h, f2_ring.mm_base + 16 * (sup - 1))
        for k in range(DCH):
            wl.inc(g.dma_start(out=wb2[:, k * 1024:k * 1024 + 512],
                               in_=w1f[k * 128:(k + 1) * 128, sup * 512:(sup + 1) * 512]), 16)
            wl.inc(g.dma_start(out=wb2[:, k * 1024 + 512:(k + 1) * 1024],
                               in_=w3f[k * 128:(k + 1) * 128, sup * 512:(sup + 1) * 512]), 16)
        t.wait_ge(wl.h, wl.v)
        for hc4 in range(4):
            hc = sup * 4 + hc4
            for tb in range(4):
                fn_blk = [fnT[:, k * TOK + tb * 512:k * TOK + (tb + 1) * 512] for k in range(DCH)]
                bank = f_ring.produce(t)
                f_ring.mm_done(mm_accum(t, bank[:], [(wb2[:, k * 1024 + hc4 * 128:k * 1024 + (hc4 + 1) * 128],
                                                      fn_blk[k]) for k in range(DCH)]))
                bg = f_ring.consume_wait(s)
                s.wait_ge(sty.h, sty.v)
                f_ring.ev_done(s.activation(gu2[:, tb * 512:(tb + 1) * 512], bg[:], AF.Silu))
                bank = f2_ring.produce(t)
                f2_ring.mm_done(mm_accum(t, bank[:], [(wb2[:, k * 1024 + 512 + hc4 * 128:k * 1024 + 512 + (hc4 + 1) * 128],
                                                       fn_blk[k]) for k in range(DCH)]))
                bu = f2_ring.consume_wait(v)
                f2_ring.ev_done(v.tensor_copy(gu2[:, TOK + tb * 512:TOK + (tb + 1) * 512], bu[:]))
                v.wait_ge(f_ring.ev.h, f_ring.ev.v)
                v.wait_ge(sty.h, sty.v)
                ev.inc(v.tensor_mul(hid2[:, tb * 512:(tb + 1) * 512],
                                    gu2[:, tb * 512:(tb + 1) * 512],
                                    gu2[:, TOK + tb * 512:TOK + (tb + 1) * 512]), 1)
                y.wait_ge(ev.h, ev.v)
                sty.inc(y.dma_start(
                    out=hid_scr[tb * 4:(tb + 1) * 4, :, hc * 128:(hc + 1) * 128]
                        .rearrange("n p q -> p n q"),
                    in_=hid2[:, tb * 512:(tb + 1) * 512]
                        .rearrange("p (n q) -> p n q", q=128)), 16)
    sc_f1.close()
    sc_f2 = ScopeAlloc(nc)
    dwn = [sc_f2.sb(f"dwn{i}", [128, 44 * 512], BF16) for i in range(2)]
    hbf = [sc_f2.sb(f"hbf{i}", [128, 44 * 128], BF16) for i in range(2)]
    outf2 = sc_f2.sb("outf2", [128, 512], F32)
    ps_d2 = [sc_f2.ps(f"ps_d2{i}", [128, 512]) for i in range(2)]
    d2_ring = Ring(p, ps_d2, "abd2")
    g.wait_ge(sty.h, sty.v)
    g.wait_ge(stg.h, stg.v)
    for db in range(4):
        dwb2 = dwn[db % 2]
        if db >= 2:
            g.wait_ge(d2_ring.mm.h, d2_ring.mm_base + 16 * (db - 1))
        for hc in range(44):
            wl.inc(g.dma_start(out=dwb2[:, hc * 512:(hc + 1) * 512],
                               in_=w2f[hc * 128:(hc + 1) * 128, db * 512:(db + 1) * 512]), 16)
        for tt in range(NTT):
            hb2 = hbf[tt % 2]
            gidx = db * 16 + tt
            if gidx >= 2:
                y.wait_ge(d2_ring.mm.h, d2_ring.mm_base + gidx - 1)
            hl.inc(y.dma_start(out=hb2[:], in_=hid_scr[tt]), 16)
            t.wait_ge(wl.h, wl.v)
            t.wait_ge(hl.h, hl.v)
            bank = d2_ring.produce(t)
            d2_ring.mm_done(mm_accum(t, bank[:], [(hb2[:, hc * 128:(hc + 1) * 128],
                                                   dwb2[:, hc * 512:(hc + 1) * 512])
                                                  for hc in range(44)]))
            v.wait_ge(stg.h, stg.v)
            bd = d2_ring.consume_wait(v)
            d2_ring.ev_done(v.tensor_copy(outf2[:], bd[:]))
            g.wait_ge(d2_ring.ev.h, d2_ring.ev.v)
            stg.inc(g.dma_start(out=xout[tt * 128:(tt + 1) * 128, db * 512:(db + 1) * 512],
                                in_=outf2[:], accum_op=ALU.add), 16)
    g.wait_ge(stg.h, stg.v)
    sc_f2.close()
    sc_f.close()
    p.barrier()
    p.compile_block()
    return p



def _chunk_major(vec):
    c = vec.shape[0] // 128
    return np.ascontiguousarray(np.asarray(vec, np.float32).reshape(c, 128).T)


def prep_l1_ffn(inputs):
    base = dict(host_consts())
    base.update({
        "wgT": np.ascontiguousarray(np.asarray(inputs["L1_ffn_wq_gate"]).T),
        "wuT": np.ascontiguousarray(np.asarray(inputs["L1_ffn_wq_up"]).T),
        "wdT": np.ascontiguousarray(np.asarray(inputs["L1_ffn_wq_down"]).T),
        "sg_c": _chunk_major(inputs["L1_ffn_s_gate"]),
        "su_c": _chunk_major(inputs["L1_ffn_s_up"]),
        "sd_flat": np.asarray(inputs["L1_ffn_s_down"], np.float32).reshape(1, DM),
    })
    return base


def shard_phaseB(x_full):
    xt = np.asarray(x_full, np.float32).reshape(B, 8, 512, DM)
    return [np.ascontiguousarray(xt[:, c].reshape(TOK, DM)) for c in range(NCORES)]


def unshard_phaseB(shards):
    out = np.empty((B, 8, 512, DM), np.float32)
    for c in range(NCORES):
        out[:, c] = shards[c].reshape(B, 512, DM)
    return out.reshape(B, SEQ, DM)


def run_l1_ffn(x_full, inputs, prog=None):
    base = prep_l1_ffn(inputs)
    if prog is None:
        prog = build_l1_ffn()
    in_maps = []
    for shard in shard_phaseB(x_full):
        m = dict(base); m["x"] = shard
        in_maps.append(m)
    res = run_bass_kernel_spmd(prog.nc, in_maps, core_ids=list(range(NCORES)))
    return unshard_phaseB([res.results[c]["xout"] for c in range(NCORES)])


_PROGS = {}

def _prog(name, builder):
    if name not in _PROGS:
        _PROGS[name] = builder()
    return _PROGS[name]


def kernel(**inputs):
    x = np.asarray(inputs["L0_attn_a_x"], np.float32)
    x = run_l0_attn_a(x, inputs, _prog("aa", build_l0_attn_a))
    x = run_l0_ffn(x, inputs, _prog("f0", build_l0_ffn))
    x = run_l0_attn_b(x, inputs, _prog("ab", build_l0_attn_b))
    x = run_l0_ffn(x, inputs, _prog("f0", build_l0_ffn))
    x = run_l1_attn_a(x, inputs, _prog("1a", build_l1_attn_a))
    x = run_l1_ffn(x, inputs, _prog("f1", build_l1_ffn))
    x = run_l1_attn_b(x, inputs, _prog("1b", build_l1_attn_b))
    x = run_l1_ffn(x, inputs, _prog("f1", build_l1_ffn))
    return x
